# version 32
# baseline (speedup 1.0000x reference)
import numpy as np
import concourse.bass as bass
import concourse.mybir as mybir
from concourse.bass_utils import run_bass_kernel_spmd

F32 = mybir.dt.float32
ALU = mybir.AluOpType
ACT = mybir.ActivationFunctionType
AX = mybir.AxisListType

D = 1024
C = 128
HID = 2816
NKC = D // 128
NIC = HID // 128
QTH = [0.25, 0.75, 1.25, 1.75, 2.5, 3.5, 5.0]


class Buf:
    __slots__ = ("name", "writer", "readers", "unread", "is_psum")

    def __init__(self, name, is_psum=False):
        self.name = name
        self.writer = None
        self.readers = []
        self.unread = False
        self.is_psum = is_psum


class Sched:

    PHYS = {"T": "T", "V": "V", "S": "S", "G": "G", "LQ": "LQ", "SQ": "S"}

    def __init__(self, nc):
        self.nc = nc
        self.sems = {}
        self.count = {}
        self.dma_key = set()
        self.hw = {}
        self.prog = {k: [] for k in ("T", "V", "S", "G", "LQ")}
        self.labels = {}

    def add_engine(self, key):
        self.sems[key] = self.nc.alloc_semaphore(name=f"sem_{key}")
        self.count[key] = 0

    def add_dma_sem(self, key):
        self.sems[key] = self.nc.alloc_semaphore(name=f"sem_{key}")
        self.count[key] = 0
        self.dma_key.add(key)

    def _wait(self, phys, sem_key, val):
        if sem_key in self.dma_key:
            val = self.count[sem_key]
        if val <= 0:
            return
        hwk = (phys, sem_key)
        if self.hw.get(hwk, 0) >= val:
            return
        sem = self.sems[sem_key]
        self.prog[phys].append(lambda e, sem=sem, val=val: e.wait_ge(sem, val))
        self.hw[hwk] = val

    def _pre(self, phys, reads, writes):
        need = {}
        for b in reads:
            if b.writer is not None:
                k, v = b.writer
                need[k] = max(need.get(k, 0), v)
        for b in writes:
            if b.writer is not None:
                k, v = b.writer
                need[k] = max(need.get(k, 0), v)
            for k, v in b.readers:
                need[k] = max(need.get(k, 0), v)
        for k in sorted(need):
            self._wait(phys, k, need[k])

    def op(self, ekey, f, reads=(), writes=(), accum=False):
        import sys
        fr = sys._getframe(1)
        label = f"{fr.f_code.co_name}:{fr.f_lineno}"
        for b in writes:
            if b.is_psum and b.unread and not accum:
                raise RuntimeError(
                    f"psum bank {b.name}: new group before evac "
                    f"(last writer {b.writer} = "
                    f"{self.labels.get(b.writer, '?')}, here {label})")
        phys = self.PHYS[ekey]
        self._pre(phys, reads, writes)
        self.count[ekey] += 1
        sem = self.sems[ekey]
        self.prog[phys].append(lambda e, f=f, sem=sem: f(e).then_inc(sem, 1))
        me = (ekey, self.count[ekey])
        self.labels[me] = label
        for b in writes:
            b.writer = me
            b.readers = []
            b.unread = True
        for b in reads:
            b.readers.append(me)
            b.unread = False

    def dma(self, qkey, slot_key, f, reads=(), writes=()):
        phys = self.PHYS[qkey]
        self._pre(phys, reads, writes)
        self.count[slot_key] += 16
        sem = self.sems[slot_key]
        self.prog[phys].append(lambda e, f=f, sem=sem: f(e).then_inc(sem, 16))
        me = (slot_key, self.count[slot_key])
        for b in writes:
            b.writer = me
            b.readers = []
            b.unread = True
        for b in reads:
            b.readers.append(me)
            b.unread = False

    def raw_wait_all(self, ekey, sem_key):
        self._wait(self.PHYS[ekey], sem_key, self.count[sem_key])

    def emit(self, block):
        def runner(phys):
            def run(eng):
                for item in self.prog[phys]:
                    item(eng)
            return run
        block.tensor(runner("T"))
        block.vector(runner("V"))
        block.scalar(runner("S"))
        block.gpsimd(runner("G"))
        block.sync(runner("LQ"))


def build_program(NB, T, FT, debug_bc=None):
    NCH = T // 128
    assert NCH % FT == 0
    NPH = NCH // FT
    FW = FT * 128
    nc = bass.Bass("TRN2", target_bir_lowering=False)

    x_in = nc.dram_tensor("x", [NB, T, D], F32, kind="ExternalInput").ap()
    y_out = nc.dram_tensor("y", [NB, T, D], F32, kind="ExternalOutput").ap()
    wcat_in = nc.dram_tensor("wcat", [D, 512], F32, kind="ExternalInput").ap()
    wab_in = nc.dram_tensor("wab", [D, 2], F32, kind="ExternalInput").ap()
    wo_in = nc.dram_tensor("wo", [C, D], F32, kind="ExternalInput").ap()
    bqkvg_in = nc.dram_tensor("bqkvg", [128, 512], F32, kind="ExternalInput").ap()
    bab_in = nc.dram_tensor("bab", [128, 2], F32, kind="ExternalInput").ap()
    bo_in = nc.dram_tensor("bo", [128, D], F32, kind="ExternalInput").ap()
    wg_in = nc.dram_tensor("wgate", [HID, D], F32, kind="ExternalInput").ap()
    wu_in = nc.dram_tensor("wup", [HID, D], F32, kind="ExternalInput").ap()
    wd_in = nc.dram_tensor("wdown", [D, HID], F32, kind="ExternalInput").ap()
    ident_in = nc.dram_tensor("ident", [128, 128], F32, kind="ExternalInput").ap()
    mgt_in = nc.dram_tensor("mask_gt", [128, 128], F32, kind="ExternalInput").ap()
    mge_in = nc.dram_tensor("mask_ge", [128, 128], F32, kind="ExternalInput").ap()
    ones1_in = nc.dram_tensor("ones1", [1, 128], F32, kind="ExternalInput").ap()

    dbg = nc.dram_tensor("dbg", [8, 128, 128], F32, kind="ExternalOutput").ap() if debug_bc else None
    wgus = nc.dram_tensor("wgus", [NIC, NKC, 128, 256], F32, kind="Internal").ap()
    wdqs = nc.dram_tensor("wdqs", [NIC, 128, D], F32, kind="Internal").ap()

    def sbt(name, shape):
        return nc.alloc_sbuf_tensor(name, shape, F32).ap()

    WCAT = sbt("WCAT", [128, NKC * 512])
    WAB = sbt("WAB", [128, NKC * 2])
    WO = sbt("WO", [128, D])
    BQKVG = sbt("BQKVG", [128, 512])
    BAB = sbt("BAB", [128, 2])
    BO = sbt("BO", [128, D])
    IDENT = sbt("IDENT", [128, 128])
    MGT = sbt("MGT", [128, 128])
    MGE = sbt("MGE", [128, 128])
    NGR = FW // 16
    ONES1 = sbt("ONES1", [1, 128])
    ZEROS = sbt("ZEROS", [128, 128])

    Y1 = sbt("Y1", [128, NCH * D])
    XQT = sbt("XQT", [128, NKC * FW])
    WSTR = sbt("WSTR", [128, 6144])
    STT = sbt("STATE_T", [128, NB * C])
    SSS = sbt("STATE_S", [128, NB * C])

    LNS = sbt("LNS", [128, D])
    LNT = sbt("LNT", [128, D])
    QKVG = sbt("QKVG", [128, 512])
    GT_ = sbt("GTILE", [128, 128])
    KTL = sbt("KTILE", [128, 128])
    KTQT = sbt("KTQT", [128, 256])
    ABP = sbt("ABP", [128, 2])
    ABROW = sbt("ABROW", [2, 128])
    DROW = sbt("DROW", [1, 128])
    GAM = sbt("GAM", [128, 128])
    GAMA = sbt("GAMA", [128, 128])
    DCOL = sbt("DCOL", [128, 1])
    DLC = sbt("DLC", [128, 1])
    NBD = sbt("NBD", [128, 1])
    NBC = sbt("NBC", [128, 1])
    XW = sbt("XW", [128, 128])
    XWT = sbt("XWT", [128, 128])
    XP = [sbt(f"XP{i}", [128, 128]) for i in range(2)]
    XPT = [sbt(f"XPT{i}", [128, 128]) for i in range(2)]
    YV = sbt("YV", [128, 128])
    VT = sbt("VT", [128, 128])
    WTT = sbt("WTT", [128, 128])
    OG = sbt("OG", [128, 128])
    OT = sbt("OT", [128, 128])
    TMP1 = sbt("TMP1", [128, 128])
    TMP2 = sbt("TMP2", [128, 128])
    SMALL = sbt("SMALLP", [128, 16])

    HBUF = sbt("HBUF", [128, 2 * FW])
    HQT = sbt("HQT", [128, 3 * FW])
    QSC = sbt("QSC", [128, 128])
    QS2 = sbt("QS2", [128, 128])
    AXT = sbt("AXT", [128, 1024])
    SGN = sbt("SGN", [128, 1024])
    WPQ = sbt("WPQ", [128, 1024])
    PAX = sbt("PAX", [128, 1024])
    PASGN = sbt("PASGN", [128, 1024])
    PASC = sbt("PASC", [128, 192])
    WPQ2 = sbt("WPQ2", [128, 1024])
    PAX2 = sbt("PAX2", [128, 1024])
    PASGN2 = sbt("PASGN2", [128, 1024])
    PASC2 = sbt("PASC2", [128, 192])
    WSTG = sbt("WSTG", [128, 512])

    PS = [nc.alloc_psum_tensor(f"ps{i}", [128, 512], F32).ap() for i in range(8)]

    s = Sched(nc)
    bufs = {}

    def B(name):
        if name not in bufs:
            bufs[name] = Buf(name, is_psum=name.startswith("ps"))
        return bufs[name]

    PSB = [B(f"ps{i}") for i in range(8)]
    xslot = [B("xslot0"), B("xslot1")]
    wslot = [B("wslot0"), B("wslot1")]
    dnslot = [B("dnslot0"), B("dnslot1")]
    y1b = [B(f"y1_{i}") for i in range(NCH)]
    xqtb = [B(f"xqt_{i}") for i in range(FT)]
    stb = [B(f"state_t{i}") for i in range(NB)]
    ssb = [B(f"state_s{i}") for i in range(NB)]
    onceb = B("once")
    wgusb = B("wgus_dram")
    wdqsb = B("wdqs_dram")

    if True:
        for k in ("T", "V", "S", "G", "LQ"):
            s.add_engine(k)
        for k in ("xs0", "xs1", "ws0", "ws1", "dns0", "dns1", "once", "wps"):
            s.add_dma_sem(k)
        for c in range(NCH):
            s.add_dma_sem(f"st{c}")
        xsem = ["xs0", "xs1"]
        wsem = ["ws0", "ws1"]

        def once_load(dst, src):
            s.dma("LQ", "once", lambda e: e.dma_start(out=dst, in_=src),
                  writes=[onceb])

        once_load(WCAT[:, :].rearrange("p (kc n) -> p kc n", n=512),
                  wcat_in.rearrange("(kc p) n -> p kc n", p=128))
        once_load(WAB[:, :].rearrange("p (kc n) -> p kc n", n=2),
                  wab_in.rearrange("(kc p) n -> p kc n", p=128))
        once_load(WO[:, :], wo_in[:, :])
        once_load(BQKVG[:, :], bqkvg_in[:, :])
        once_load(BAB[:, :], bab_in[:, :])
        once_load(BO[:, :], bo_in[:, :])
        once_load(IDENT[:, :], ident_in[:, :])
        once_load(MGT[:, :], mgt_in[:, :])
        once_load(MGE[:, :], mge_in[:, :])
        once_load(ONES1[:, :], ones1_in[:, :])
        s.op("V", lambda e: e.memset(ZEROS[:, :], 0.0), writes=[onceb])
        for b in range(NB):
            s.op("V", lambda e, b=b: e.memset(STT[:, b * C:(b + 1) * C], 0.0),
                 writes=[stb[b]])
            s.op("V", lambda e, b=b: e.memset(SSS[:, b * C:(b + 1) * C], 0.0),
                 writes=[ssb[b]])

        def tgroup(srcs, bank, dsts):
            n = len(srcs)
            for i, (ap, rb) in enumerate(srcs):
                s.op("T", lambda e, i=i, ap=ap, last=(i == n - 1): e.matmul(
                    PS[bank][:, i * 128:(i + 1) * 128], ap, IDENT[:, :],
                    is_transpose=True, start=(i == 0), stop=last),
                    reads=list(rb) + [onceb], writes=[PSB[bank]],
                    accum=(i > 0))
            for (ap, wb, ek, lo, hi) in dsts:
                if ek == "S":
                    s.op("S", lambda e, ap=ap, lo=lo, hi=hi: e.copy(
                        ap, PS[bank][:, lo:hi]), reads=[PSB[bank]], writes=list(wb))
                else:
                    s.op(ek, lambda e, ap=ap, lo=lo, hi=hi: e.tensor_copy(
                        ap, PS[bank][:, lo:hi]), reads=[PSB[bank]], writes=list(wb))

        def layernorm(x_ap, x_bufs, out_ap, out_bufs, sq_ap=None, sq_buf=None,
                      small0=0):
            m = SMALL[:, small0:small0 + 1]
            vv = SMALL[:, small0 + 1:small0 + 2]
            r0 = SMALL[:, small0 + 2:small0 + 3]
            t1 = SMALL[:, small0 + 3:small0 + 4]
            smb = B(f"smallpool{small0}")
            if sq_ap is None:
                sq_ap, sq_buf = LNT[:, :], B("lnt")
            ctb, sqb = out_bufs[0], sq_buf
            s.op("V", lambda e: e.reduce_sum(m, x_ap, AX.X), reads=list(x_bufs), writes=[smb])
            s.op("V", lambda e: e.tensor_single_scalar(m, m, 1.0 / D, ALU.mult),
                 reads=[smb], writes=[smb])
            s.op("V", lambda e: e.tensor_single_scalar(out_ap, x_ap, m, ALU.subtract),
                 reads=list(x_bufs) + [smb], writes=[ctb])
            s.op("S", lambda e: e.activation(sq_ap, out_ap, ACT.Square, accum_out=vv),
                 reads=[ctb], writes=[sqb, smb])
            s.op("V", lambda e: e.tensor_scalar(vv, vv, 1.0 / D, 1e-5, ALU.mult, ALU.add),
                 reads=[smb], writes=[smb])
            s.op("S", lambda e: e.activation(t1, vv, ACT.Sqrt), reads=[smb], writes=[smb])
            s.op("V", lambda e: e.reciprocal(r0, t1), reads=[smb], writes=[smb])
            s.op("V", lambda e: e.tensor_single_scalar(out_ap, out_ap, r0, ALU.mult),
                 reads=[smb] + list(out_bufs), writes=list(out_bufs))

        def quant_gen(x_ap, x_bufs, out_ap, out_bufs, width):
            assert width <= 1024 and width % 32 == 0
            gb = width // 16
            qb = B("quantpool")
            axb, sgb = B("axt"), B("sgn")
            ob = out_bufs[0]
            s.op("V", lambda e: e.tensor_reduce(
                QSC[:, 0:gb], x_ap.rearrange("p (g e) -> p g e", e=16),
                AX.X, ALU.max, apply_absolute_value=True),
                reads=list(x_bufs), writes=[qb])
            yield
            c6 = 0.16666667163372039794921875
            s.op("V", lambda e: e.tensor_single_scalar(QS2[:, 0:gb], QSC[:, 0:gb], c6, ALU.mult),
                 reads=[qb], writes=[qb])
            s.op("V", lambda e: e.tensor_single_scalar(QS2[:, 64:64 + gb], QS2[:, 0:gb], 6.0, ALU.mult),
                 reads=[qb], writes=[qb])
            yield
            s.op("V", lambda e: e.tensor_tensor(QS2[:, 64:64 + gb], QSC[:, 0:gb], QS2[:, 64:64 + gb], ALU.subtract),
                 reads=[qb], writes=[qb])
            s.op("V", lambda e: e.tensor_single_scalar(QS2[:, 64:64 + gb], QS2[:, 64:64 + gb], c6, ALU.mult),
                 reads=[qb], writes=[qb])
            yield
            s.op("V", lambda e: e.tensor_tensor(QSC[:, 0:gb], QS2[:, 0:gb], QS2[:, 64:64 + gb], ALU.add),
                 reads=[qb], writes=[qb])
            s.op("V", lambda e: e.tensor_single_scalar(QSC[:, 0:gb], QSC[:, 0:gb], 1e-12, ALU.max),
                 reads=[qb], writes=[qb])
            yield
            s.op("V", lambda e: e.reciprocal(QS2[:, 0:gb], QSC[:, 0:gb]),
                 reads=[qb], writes=[qb])
            s.op("V", lambda e: e.tensor_single_scalar(QSC[:, 0:gb], QSC[:, 0:gb], 0.5, ALU.mult),
                 reads=[qb], writes=[qb])
            yield
            s.op("S", lambda e: e.activation(AXT[:, 0:width], x_ap, ACT.Abs),
                 reads=list(x_bufs), writes=[axb])
            yield
            s.op("S", lambda e: e.sign(SGN[:, 0:width], x_ap),
                 reads=list(x_bufs), writes=[sgb])
            yield
            rb = QS2[:, 0:gb].unsqueeze(2).broadcast_to([128, gb, 16])
            s.op("G", lambda e: e.tensor_tensor(
                AXT[:, 0:width].rearrange("p (g e) -> p g e", e=16),
                AXT[:, 0:width].rearrange("p (g e) -> p g e", e=16), rb, ALU.mult),
                reads=[axb, qb], writes=[axb])
            yield
            u = AXT[:, 0:width]
            v = out_ap
            s.op("V", lambda e: e.tensor_scalar(v, u, QTH[6], 2.0, ALU.is_gt, ALU.mult),
                 reads=[axb], writes=[ob])
            yield
            for th in (QTH[5], QTH[4]):
                s.op("V", lambda e, th=th: e.scalar_tensor_tensor(
                    v, u, th, v, ALU.is_gt, ALU.add), reads=[axb, ob], writes=[ob])
                yield
            s.op("V", lambda e: e.tensor_single_scalar(v, v, 2.0, ALU.mult),
                 reads=[ob], writes=[ob])
            yield
            for th in (QTH[3], QTH[2], QTH[1], QTH[0]):
                s.op("V", lambda e, th=th: e.scalar_tensor_tensor(
                    v, u, th, v, ALU.is_gt, ALU.add), reads=[axb, ob], writes=[ob])
                yield
            sb_ = QSC[:, 0:gb].unsqueeze(2).broadcast_to([128, gb, 16])
            s.op("G", lambda e: e.tensor_tensor(
                out_ap.rearrange("p (g e) -> p g e", e=16),
                out_ap.rearrange("p (g e) -> p g e", e=16), sb_, ALU.mult),
                reads=[qb, ob], writes=[ob])
            yield
            s.op("G", lambda e: e.tensor_tensor(out_ap, out_ap, SGN[:, 0:width], ALU.mult),
                 reads=[sgb, ob], writes=[ob])

        def qchain_gen(x_ap, x_bufs, out_ap, out_bufs, width, u_ap, u_buf,
                       sg_ap, sg_buf, sc0, pool, q_aps=None):
            gb = width // 16
            u_ap = u_ap[:, 0:width]
            sg_ap = sg_ap[:, 0:width]
            qb = B(pool)
            ob = out_bufs[0]
            if q_aps is None:
                Q0 = QSC[:, sc0:sc0 + gb]
                Q1 = QS2[:, sc0:sc0 + gb]
                Q2 = QS2[:, sc0 + 32:sc0 + 32 + gb]
            else:
                Q0, Q1, Q2 = q_aps
            s.op("V", lambda e: e.tensor_reduce(
                Q0, x_ap.rearrange("p (g e) -> p g e", e=16),
                AX.X, ALU.max, apply_absolute_value=True),
                reads=list(x_bufs), writes=[qb])
            yield
            c6 = 0.16666667163372039794921875
            s.op("V", lambda e: e.tensor_single_scalar(Q1, Q0, c6, ALU.mult),
                 reads=[qb], writes=[qb])
            s.op("S", lambda e: e.activation(u_ap, x_ap, ACT.Abs),
                 reads=list(x_bufs), writes=[u_buf])
            yield
            s.op("S", lambda e: e.sign(sg_ap, x_ap), reads=list(x_bufs), writes=[sg_buf])
            s.op("V", lambda e: e.tensor_single_scalar(Q2, Q1, 6.0, ALU.mult),
                 reads=[qb], writes=[qb])
            yield
            s.op("V", lambda e: e.tensor_tensor(Q2, Q0, Q2, ALU.subtract),
                 reads=[qb], writes=[qb])
            s.op("V", lambda e: e.tensor_single_scalar(Q2, Q2, c6, ALU.mult),
                 reads=[qb], writes=[qb])
            yield
            s.op("V", lambda e: e.tensor_tensor(Q0, Q1, Q2, ALU.add),
                 reads=[qb], writes=[qb])
            s.op("V", lambda e: e.tensor_single_scalar(Q0, Q0, 1e-12, ALU.max),
                 reads=[qb], writes=[qb])
            yield
            s.op("V", lambda e: e.reciprocal(Q1, Q0), reads=[qb], writes=[qb])
            s.op("V", lambda e: e.tensor_single_scalar(Q0, Q0, 0.5, ALU.mult),
                 reads=[qb], writes=[qb])
            yield
            rb = Q1.unsqueeze(2).broadcast_to([128, gb, 16])
            s.op("G", lambda e: e.tensor_tensor(
                u_ap.rearrange("p (g e) -> p g e", e=16),
                u_ap.rearrange("p (g e) -> p g e", e=16), rb, ALU.mult),
                reads=[u_buf, qb], writes=[u_buf])
            yield
            s.op("V", lambda e: e.tensor_scalar(out_ap, u_ap, QTH[6], 2.0, ALU.is_gt, ALU.mult),
                 reads=[u_buf], writes=[ob])
            yield
            for th in (QTH[5], QTH[4]):
                s.op("V", lambda e, th=th: e.scalar_tensor_tensor(
                    out_ap, u_ap, th, out_ap, ALU.is_gt, ALU.add), reads=[u_buf, ob], writes=[ob])
                yield
            s.op("V", lambda e: e.tensor_single_scalar(out_ap, out_ap, 2.0, ALU.mult),
                 reads=[ob], writes=[ob])
            yield
            for th in (QTH[3], QTH[2], QTH[1], QTH[0]):
                s.op("V", lambda e, th=th: e.scalar_tensor_tensor(
                    out_ap, u_ap, th, out_ap, ALU.is_gt, ALU.add), reads=[u_buf, ob], writes=[ob])
                yield
            sb_ = Q0.unsqueeze(2).broadcast_to([128, gb, 16])
            s.op("G", lambda e: e.tensor_tensor(
                out_ap.rearrange("p (g e) -> p g e", e=16),
                out_ap.rearrange("p (g e) -> p g e", e=16), sb_, ALU.mult),
                reads=[qb, ob], writes=list(out_bufs))
            yield
            s.op("G", lambda e: e.tensor_tensor(out_ap, out_ap, sg_ap, ALU.mult),
                 reads=[sg_buf, ob], writes=list(out_bufs))

        def quant(x_ap, x_bufs, out_ap, out_bufs, width):
            for _ in quant_gen(x_ap, x_bufs, out_ap, out_bufs, width):
                pass

        wpsets = [
            dict(ub="axtA", sgb="sgnA", pool="wqpoolA", bank=3,
                 u=AXT[:, 0:1024], sg=SGN[:, 0:1024],
                 q=((QSC, 0), (QS2, 0), (QS2, 64)),
                 stg=WSTG[:, 0:512], stgb="wstg"),
            dict(ub="pax", sgb="pasgn", pool="papool", bank=5,
                 u=PAX[:, 0:1024], sg=PASGN[:, 0:1024],
                 q=((PASC, 0), (PASC, 64), (PASC, 128)),
                 stg=WPQ[:, 0:512], stgb="wpq"),
        ]

        def wp_chain(st, sl, slb, pw):
            gb = pw // 16
            qaps = tuple(t[:, o:o + gb] for (t, o) in st["q"])
            return qchain_gen(sl, [slb], sl, [slb], pw,
                              st["u"], B(st["ub"]), st["sg"], B(st["sgb"]),
                              0, st["pool"], q_aps=qaps)

        def wprep_gateup_gen(k, src_row0, which_in, half):
            st = wpsets[k % 2]
            slot = k % 2
            sl = WSTR[:, slot * 2048: slot * 2048 + D]
            slb = wslot[slot]
            s.dma("LQ", wsem[slot], lambda e, sl=sl: e.dma_start(
                out=sl, in_=which_in[src_row0:src_row0 + 128, :]), writes=[slb])
            yield
            yield from wp_chain(st, sl, slb, D)
            yield
            nt = src_row0 // 128
            for g in range(2):
                tgroup([(sl[:, (g * 4 + i) * 128:(g * 4 + i + 1) * 128], [slb])
                        for i in range(4)], st["bank"],
                       [(st["stg"], [B(st["stgb"])], "S", 0, 512)])
                yield
                for i in range(4):
                    kc = g * 4 + i
                    s.dma("SQ", "wps", lambda e, kc=kc, i=i, nt=nt, st=st,
                          half=half: e.dma_start(
                        out=wgus[nt, kc, :, half * 128:(half + 1) * 128],
                        in_=st["stg"][:, i * 128:(i + 1) * 128]),
                        reads=[B(st["stgb"])], writes=[wgusb])
                yield

        def wprep_down_gen(k, nt, p0, pw):
            st = wpsets[k % 2]
            slot = k % 2
            sl = WSTR[:, slot * 2048: slot * 2048 + pw]
            slb = wslot[slot]
            s.dma("LQ", wsem[slot], lambda e, sl=sl, p0=p0, pw=pw: e.dma_start(
                out=sl, in_=wd_in[nt * 128:(nt + 1) * 128, p0:p0 + pw]),
                writes=[slb])
            yield
            yield from wp_chain(st, sl, slb, pw)
            yield
            nics = pw // 128
            for g in range((nics + 3) // 4):
                ii = list(range(g * 4, min(g * 4 + 4, nics)))
                tgroup([(sl[:, i * 128:(i + 1) * 128], [slb]) for i in ii],
                       st["bank"],
                       [(st["stg"][:, 0:len(ii) * 128], [B(st["stgb"])], "S",
                         0, len(ii) * 128)])
                yield
                for j, i in enumerate(ii):
                    ic = p0 // 128 + i
                    s.dma("SQ", "wps", lambda e, ic=ic, j=j, nt=nt,
                          st=st: e.dma_start(
                        out=wdqs[ic, :, nt * 128:(nt + 1) * 128],
                        in_=st["stg"][:, j * 128:(j + 1) * 128]),
                        reads=[B(st["stgb"])], writes=[wdqsb])
                yield

        wprep_units = []
        for nt in range(NIC):
            wprep_units.append(("gu", nt * 128, wg_in, 0))
            wprep_units.append(("gu", nt * 128, wu_in, 1))
        for nt in range(NKC):
            for (p0, pw) in ((0, 1024), (1024, 1024), (2048, 768)):
                wprep_units.append(("dn", nt, p0, pw))

        chunk_no = [0]

        def attn_chunk_gen(b, c):
            slot = chunk_no[0] % 2
            chunk_no[0] += 1
            xa = Y1[:, c * D:(c + 1) * D]
            xb = y1b[c]
            s.dma("LQ", xsem[slot], lambda e: e.dma_start(
                out=xa, in_=x_in[b, c * 128:(c + 1) * 128, :]), writes=[xb])

            lnb = B("lns")
            layernorm(xa, [xb], LNS[:, :], [lnb])
            yield

            for g in range(2):
                tgroup([(LNS[:, (g * 4 + i) * 128:(g * 4 + i + 1) * 128], [lnb])
                        for i in range(4)], 7,
                       [(LNT[:, g * 512:(g + 1) * 512], [B("lnt")], "S", 0, 512)])
            yield

            for kc in range(NKC):
                s.op("T", lambda e, kc=kc: e.matmul(
                    PS[6][:, 0:512], LNT[:, kc * 128:(kc + 1) * 128],
                    WCAT[:, kc * 512:(kc + 1) * 512],
                    start=(kc == 0), stop=(kc == NKC - 1)),
                    reads=[B("lnt"), onceb], writes=[PSB[6]], accum=(kc > 0))
                yield
            for kc in range(NKC):
                s.op("T", lambda e, kc=kc: e.matmul(
                    PS[7][:, 384:386], LNT[:, kc * 128:(kc + 1) * 128],
                    WAB[:, kc * 2:(kc + 1) * 2],
                    start=(kc == 0), stop=(kc == NKC - 1)),
                    reads=[B("lnt"), onceb], writes=[PSB[7]], accum=(kc > 0))
                yield

            s.op("V", lambda e: e.tensor_tensor(QKVG[:, :], PS[6][:, 0:512], BQKVG[:, :], ALU.add),
                 reads=[PSB[6], onceb], writes=[B("qkvg")])
            s.op("S", lambda e: e.activation(ABP[:, 0:1], PS[7][:, 384:385], ACT.Sigmoid,
                                             bias=BAB[:, 0:1]),
                 reads=[PSB[7], onceb], writes=[B("abp")])
            s.op("S", lambda e: e.activation(ABP[:, 1:2], PS[7][:, 385:386], ACT.Sigmoid,
                                             bias=BAB[:, 1:2]),
                 reads=[PSB[7], onceb], writes=[B("abp")])
            s.op("S", lambda e: e.activation(GT_[:, :], QKVG[:, 384:512], ACT.Sigmoid),
                 reads=[B("qkvg")], writes=[B("gtile")])
            yield

            smb = B("smallpool")
            s.op("S", lambda e: e.activation(TMP1[:, :], QKVG[:, 128:256], ACT.Square,
                                             accum_out=SMALL[:, 4:5]),
                 reads=[B("qkvg")], writes=[B("tmp1"), smb])
            s.op("V", lambda e: e.tensor_single_scalar(SMALL[:, 4:5], SMALL[:, 4:5], 1e-38, ALU.max),
                 reads=[smb], writes=[smb])
            s.op("S", lambda e: e.activation(SMALL[:, 5:6], SMALL[:, 4:5], ACT.Sqrt),
                 reads=[smb], writes=[smb])
            s.op("V", lambda e: e.reciprocal(SMALL[:, 6:7], SMALL[:, 5:6]),
                 reads=[smb], writes=[smb])
            s.op("V", lambda e: e.tensor_single_scalar(SMALL[:, 6:7], SMALL[:, 6:7], 1e12, ALU.min),
                 reads=[smb], writes=[smb])
            s.op("V", lambda e: e.tensor_single_scalar(KTL[:, :], QKVG[:, 128:256], SMALL[:, 6:7], ALU.mult),
                 reads=[B("qkvg"), smb], writes=[B("ktile")])

            tgroup([(KTL[:, :], [B("ktile")]), (QKVG[:, 0:128], [B("qkvg")])], 7,
                   [(KTQT[:, 0:256], [B("ktqt")], "S", 0, 256)])
            yield

            s.op("T", lambda e: e.matmul(PS[6][0:2, 128:256], ABP[:, :], IDENT[:, :],
                                         is_transpose=True, start=True, stop=True),
                 reads=[B("abp"), onceb], writes=[PSB[6]])
            yield
            s.op("S", lambda e: e.copy(ABROW[:, :], PS[6][0:2, 128:256]),
                 reads=[PSB[6]], writes=[B("abrow")])
            s.op("V", lambda e: e.tensor_tensor_scan(
                DROW[:, :], ABROW[0:1, :], ZEROS[0:1, :], 1.0, ALU.mult, ALU.add),
                reads=[B("abrow"), onceb], writes=[B("drow")])
            s.op("T", lambda e: e.matmul(PS[6][:, 256:384], ONES1[:, :], ABROW[0:1, :],
                                         start=True, stop=True),
                 reads=[B("abrow"), onceb], writes=[PSB[6]])
            yield
            s.op("V", lambda e: e.tensor_single_scalar(TMP2[:, :], PS[6][:, 256:384], 1.0, ALU.subtract),
                 reads=[PSB[6]], writes=[B("tmp2")])
            s.op("V", lambda e: e.tensor_tensor(TMP2[:, :], TMP2[:, :], MGT[:, :], ALU.mult),
                 reads=[B("tmp2"), onceb], writes=[B("tmp2")])
            s.op("V", lambda e: e.tensor_single_scalar(TMP2[:, :], TMP2[:, :], 1.0, ALU.add),
                 reads=[B("tmp2")], writes=[B("tmp2")])
            s.op("V", lambda e: e.tensor_tensor_scan(
                GAM[:, :], TMP2[:, :], ZEROS[:, :], 1.0, ALU.mult, ALU.add),
                reads=[B("tmp2"), onceb], writes=[B("gam")])
            s.op("G", lambda e: e.tensor_tensor(GAM[:, :], GAM[:, :], MGE[:, :], ALU.mult),
                 reads=[B("gam"), onceb], writes=[B("gam")])
            s.op("T", lambda e: e.matmul(PS[6][:, 384:385], ONES1[:, :], DROW[0:1, 127:128],
                                         start=True, stop=True),
                 reads=[B("drow"), onceb], writes=[PSB[6]])
            yield
            s.op("S", lambda e: e.copy(DLC[:, :], PS[6][:, 384:385]),
                 reads=[PSB[6]], writes=[B("dlc")])
            s.op("T", lambda e: e.matmul(PS[7][:, 0:128], GAM[:, :], IDENT[:, :],
                                         is_transpose=True, start=True, stop=True),
                 reads=[B("gam"), onceb], writes=[PSB[7]])
            yield
            s.op("V", lambda e: e.tensor_tensor(GAMA[:, :], PS[7][:, 0:128], IDENT[:, :], ALU.subtract),
                 reads=[PSB[7], onceb], writes=[B("gama")])
            s.op("T", lambda e: e.matmul(PS[7][:, 0:1], DROW[:, :], IDENT[0:1, 0:1],
                                         is_transpose=True, start=True, stop=True),
                 reads=[B("drow"), onceb], writes=[PSB[7]])
            yield
            s.op("S", lambda e: e.copy(DCOL[:, :], PS[7][:, 0:1]),
                 reads=[PSB[7]], writes=[B("dcol")])
            s.op("V", lambda e: e.tensor_scalar(NBD[:, :], DCOL[:, :], ABP[:, 1:2], -1.0, ALU.mult, ALU.mult),
                 reads=[B("dcol"), B("abp")], writes=[B("nbd")])
            s.op("V", lambda e: e.tensor_single_scalar(NBC[:, :], ABP[:, 1:2], -1.0, ALU.mult),
                 reads=[B("abp")], writes=[B("nbc")])
            yield

            s.op("T", lambda e: e.matmul(PS[7][:, 0:128], KTQT[:, 0:128], KTQT[:, 0:128],
                                         start=True, stop=True),
                 reads=[B("ktqt")], writes=[PSB[7]])
            yield
            s.op("V", lambda e: e.tensor_tensor(XW[:, :], PS[7][:, 0:128], GAMA[:, :], ALU.mult),
                 reads=[PSB[7], B("gama")], writes=[B("xw")])
            s.op("V", lambda e: e.tensor_single_scalar(XW[:, :], XW[:, :], NBC[:, :], ALU.mult),
                 reads=[B("xw"), B("nbc")], writes=[B("xw")])
            s.op("T", lambda e: e.matmul(PS[7][:, 0:128], XW[:, :], IDENT[:, :],
                                         is_transpose=True, start=True, stop=True),
                 reads=[B("xw"), onceb], writes=[PSB[7]])
            yield
            s.op("S", lambda e: e.copy(XWT[:, :], PS[7][:, 0:128]),
                 reads=[PSB[7]], writes=[B("xwt")])

            s.op("T", lambda e: e.matmul(PS[6][:, 0:128], KTQT[:, 0:128], STT[:, b * C:(b + 1) * C],
                                         start=True, stop=True),
                 reads=[B("ktqt"), stb[b]], writes=[PSB[6]])
            yield
            s.op("V", lambda e: e.tensor_single_scalar(TMP1[:, :], QKVG[:, 256:384], ABP[:, 1:2], ALU.mult),
                 reads=[B("qkvg"), B("abp")], writes=[B("tmp1")])
            s.op("V", lambda e: e.scalar_tensor_tensor(YV[:, :], PS[6][:, 0:128], NBD[:, :],
                                                       TMP1[:, :], ALU.mult, ALU.add),
                 reads=[PSB[6], B("nbd"), B("tmp1")], writes=[B("yv")])
            yield

            cx, cxt = B("xw"), B("xwt")
            cx_ap, cxt_ap = XW, XWT
            for lev in range(7):
                pb = 7 if lev % 2 == 0 else 6
                pb2 = 6 if lev % 2 == 0 else 7
                s.op("T", lambda e, pb=pb, cxt_ap=cxt_ap: e.matmul(
                    PS[pb][:, 0:128], cxt_ap[:, :], YV[:, :], start=True, stop=True),
                    reads=[cxt, B("yv")], writes=[PSB[pb]])
                yield
                s.op("V", lambda e, pb=pb: e.tensor_tensor(YV[:, :], YV[:, :], PS[pb][:, 0:128], ALU.add),
                     reads=[PSB[pb], B("yv")], writes=[B("yv")])
                if lev < 6:
                    par = lev % 2
                    nx, nxt = B(f"xp{par}"), B(f"xpt{par}")
                    nx_ap, nxt_ap = XP[par], XPT[par]
                    s.op("T", lambda e, pb2=pb2, cxt_ap=cxt_ap, cx_ap=cx_ap: e.matmul(
                        PS[pb2][:, 128:256], cxt_ap[:, :], cx_ap[:, :], start=True, stop=True),
                        reads=[cxt, cx], writes=[PSB[pb2]])
                    yield
                    s.op("S", lambda e, pb2=pb2, nx_ap=nx_ap: e.copy(nx_ap[:, :], PS[pb2][:, 128:256]),
                         reads=[PSB[pb2]], writes=[nx])
                    s.op("T", lambda e, pb2=pb2, cxt_ap=cxt_ap, cx_ap=cx_ap: e.matmul(
                        PS[pb2][:, 256:384], cx_ap[:, :], cxt_ap[:, :], start=True, stop=True),
                        reads=[cxt, cx], writes=[PSB[pb2]])
                    yield
                    s.op("S", lambda e, pb2=pb2, nxt_ap=nxt_ap: e.copy(nxt_ap[:, :], PS[pb2][:, 256:384]),
                         reads=[PSB[pb2]], writes=[nxt])
                    cx, cxt, cx_ap, cxt_ap = nx, nxt, nx_ap, nxt_ap
                if lev % 2 == 1:
                    yield

            s.op("T", lambda e: e.matmul(PS[7][:, 0:128], YV[:, :], IDENT[:, :],
                                         is_transpose=True, start=True, stop=True),
                 reads=[B("yv"), onceb], writes=[PSB[7]])
            yield
            s.op("S", lambda e: e.copy(VT[:, :], PS[7][:, 0:128]),
                 reads=[PSB[7]], writes=[B("vt")])
            yield

            s.op("T", lambda e: e.matmul(PS[7][:, 0:128], VT[:, :], KTQT[:, 128:256],
                                         start=True, stop=True),
                 reads=[B("vt"), B("ktqt")], writes=[PSB[7]])
            yield
            s.op("V", lambda e: e.tensor_tensor(WTT[:, :], PS[7][:, 0:128], GAM[:, :], ALU.mult),
                 reads=[PSB[7], B("gam")], writes=[B("wtt")])

            s.op("T", lambda e: e.matmul(PS[6][:, 0:128], WTT[:, :], KTL[:, :],
                                         start=True, stop=True),
                 reads=[B("wtt"), B("ktile")], writes=[PSB[6]])
            yield
            s.op("T", lambda e: e.matmul(PS[7][:, 128:256], KTQT[:, 128:256], SSS[:, b * C:(b + 1) * C],
                                         start=True, stop=True),
                 reads=[B("ktqt"), ssb[b]], writes=[PSB[7]])
            yield
            s.op("V", lambda e: e.tensor_single_scalar(OG[:, :], PS[7][:, 128:256], DCOL[:, :], ALU.mult),
                 reads=[PSB[7], B("dcol")], writes=[B("og")])
            s.op("V", lambda e: e.tensor_tensor(OG[:, :], OG[:, :], PS[6][:, 0:128], ALU.add),
                 reads=[PSB[6], B("og")], writes=[B("og")])
            s.op("V", lambda e: e.tensor_tensor(OG[:, :], OG[:, :], GT_[:, :], ALU.mult),
                 reads=[B("og"), B("gtile")], writes=[B("og")])
            yield

            s.op("V", lambda e: e.tensor_single_scalar(TMP1[:, :], YV[:, :], GAM[:, 127:128], ALU.mult),
                 reads=[B("yv"), B("gam")], writes=[B("tmp1")])
            s.op("T", lambda e: e.matmul(PS[6][:, 128:256], KTL[:, :], TMP1[:, :],
                                         start=True, stop=True),
                 reads=[B("ktile"), B("tmp1")], writes=[PSB[6]])
            yield
            s.op("V", lambda e: e.scalar_tensor_tensor(
                STT[:, b * C:(b + 1) * C], STT[:, b * C:(b + 1) * C], DLC[:, :],
                PS[6][:, 128:256], ALU.mult, ALU.add),
                reads=[PSB[6], B("dlc"), stb[b]], writes=[stb[b]])
            s.op("T", lambda e: e.matmul(PS[6][:, 256:384], TMP1[:, :], KTL[:, :],
                                         start=True, stop=True),
                 reads=[B("ktile"), B("tmp1")], writes=[PSB[6]])
            yield
            s.op("V", lambda e: e.scalar_tensor_tensor(
                SSS[:, b * C:(b + 1) * C], SSS[:, b * C:(b + 1) * C], DLC[:, :],
                PS[6][:, 256:384], ALU.mult, ALU.add),
                reads=[PSB[6], B("dlc"), ssb[b]], writes=[ssb[b]])
            yield

            if debug_bc == (b, c):
                for di, (ap_, bufs_) in enumerate([
                        (QKVG[:, 0:128], [B("qkvg")]), (KTL[:, :], [B("ktile")]),
                        (GAM[:, :], [B("gam")]), (YV[:, :], [B("yv")]),
                        (OG[:, :], [B("og")]), (GT_[:, :], [B("gtile")]),
                        (STT[:, b * C:(b + 1) * C], [stb[b]]),
                        (SSS[:, b * C:(b + 1) * C], [ssb[b]])]):
                    s.dma("SQ", "wps", lambda e, di=di, ap_=ap_: e.dma_start(
                        out=dbg[di], in_=ap_), reads=list(bufs_))
            s.op("T", lambda e: e.matmul(PS[7][:, 0:128], OG[:, :], IDENT[:, :],
                                         is_transpose=True, start=True, stop=True),
                 reads=[B("og"), onceb], writes=[PSB[7]])
            yield
            s.op("S", lambda e: e.copy(OT[:, :], PS[7][:, 0:128]),
                 reads=[PSB[7]], writes=[B("ot")])
            for h in range(2):
                pb = 6 + h
                s.op("T", lambda e, pb=pb, h=h: e.matmul(
                    PS[pb][:, 0:512], OT[:, :], WO[:, h * 512:(h + 1) * 512],
                    start=True, stop=True),
                    reads=[B("ot"), onceb], writes=[PSB[pb]])
                yield
                s.op("V", lambda e, pb=pb, h=h: e.tensor_tensor(
                    LNT[:, 0:512], PS[pb][:, 0:512], BO[:, h * 512:(h + 1) * 512], ALU.add),
                    reads=[PSB[pb], onceb], writes=[B("lnt")])
                s.op("V", lambda e, h=h, c=c: e.tensor_tensor(
                    Y1[:, c * D + h * 512: c * D + (h + 1) * 512],
                    LNT[:, 0:512],
                    Y1[:, c * D + h * 512: c * D + (h + 1) * 512], ALU.add),
                    reads=[B("lnt"), y1b[c]], writes=[y1b[c]])

        ic_no = [0]

        pasets = [
            dict(x=WPQ, xb="wpq", u=PAX, ub="pax", sg=PASGN, sgb="pasgn",
                 q=PASC, pool="papool", sm=4),
            dict(x=WPQ2, xb="wpq2", u=PAX2, ub="pax2", sg=PASGN2,
                 sgb="pasgn2", q=PASC2, pool="papool2", sm=8),
        ]

        def pa_chunk_gen(i, c):
            st = pasets[i % 2]
            xqt3 = XQT[:, :].rearrange("p (kc r) -> p kc r", r=FW)
            lnb = B(st["xb"])
            layernorm(Y1[:, c * D:(c + 1) * D], [y1b[c]], st["x"][:, :], [lnb],
                      sq_ap=st["sg"][:, :], sq_buf=B(st["sgb"]), small0=st["sm"])
            yield
            yield from qchain_gen(
                st["x"][:, :], [lnb], st["x"][:, :], [lnb], D,
                st["u"], B(st["ub"]), st["sg"], B(st["sgb"]), 0, st["pool"],
                q_aps=(st["q"][:, 0:64], st["q"][:, 64:128], st["q"][:, 128:192]))
            yield "TG"
            for g in range(2):
                tgroup([(st["x"][:, (g * 4 + j) * 128:(g * 4 + j + 1) * 128], [lnb])
                        for j in range(4)], 5,
                       [(xqt3[:, g * 4:(g + 1) * 4, i * 128:(i + 1) * 128],
                         [xqtb[i]], "S", 0, 512)])
                yield

        def phaseA_gen(b2, ph2):
            t0b = ph2 * FT
            gens = [pa_chunk_gen(i, t0b + i) for i in range(min(2, FT))]
            nxt_i = 2
            hold = []
            while gens:
                for g in list(gens):
                    r = next(g, "END")
                    if r == "TG":
                        gens.remove(g)
                        hold.append(g)
                        continue
                    if r == "END":
                        gens.remove(g)
                yield
            yield "TGREADY"
            for g in hold:
                for _ in g:
                    yield
            gens = [pa_chunk_gen(i, t0b + i) for i in range(2, FT)]
            while gens:
                for g in list(gens):
                    if next(g, "END") == "END":
                        gens.remove(g)
                yield

        def ffn_phase(b, ph, nxt_fn=None, inter=(), tick=None):
            if tick is None:
                tick = lambda n=1: 0
            t0 = ph * FT

            gph = ic_no[0]
            ic_no[0] += 1

            def par(ic):
                return (gph * NIC + ic) % 2

            def emit_load_gu(ic, pi=None):
                p_ = par(ic if pi is None else pi)
                gu = WSTR[:, p_ * 2048:(p_ + 1) * 2048]
                s.dma("LQ", wsem[p_], lambda e, ic=ic, gu=gu: e.dma_start(
                    out=gu.rearrange("p (kc n) -> p kc n", n=256),
                    in_=wgus[ic, :, :, :].rearrange("kc p n -> p kc n")),
                    reads=[wgusb], writes=[wslot[p_]])

            def emit_load_dn(ic):
                p_ = par(ic)
                dn = WSTR[:, 4096 + p_ * 1024: 4096 + (p_ + 1) * 1024]
                s.dma("LQ", f"dns{p_}", lambda e, ic=ic, dn=dn: e.dma_start(
                    out=dn, in_=wdqs[ic, :, :]),
                    reads=[wdqsb], writes=[dnslot[p_]])

            def emit_gateup(ic, zs=None, pi=None):
                p_ = par(ic if pi is None else pi)
                gu = WSTR[:, p_ * 2048:(p_ + 1) * 2048]
                hb = B(f"hbuf{p_}_lo")
                hsl = HBUF[:, p_ * FW:(p_ + 1) * FW]
                xr = [xqtb[i] for i in range(FT)]
                for half, bank in ((0, 0), (1, 1)):
                    for kc in range(NKC):
                        s.op("T", lambda e, kc=kc, gu=gu, bank=bank, half=half: e.matmul(
                            PS[bank][:, 0:FW],
                            gu[:, kc * 256 + half * 128: kc * 256 + half * 128 + 128],
                            XQT[:, kc * FW:(kc + 1) * FW],
                            start=(kc == 0), stop=(kc == NKC - 1)),
                            reads=xr + [wslot[p_]], writes=[PSB[bank]], accum=(kc > 0))
                        if zs is not None:
                            zs()
                        if kc % 2 == 1:
                            tick(1)
                s.op("S", lambda e, hsl=hsl: e.activation(hsl, PS[0][:, 0:FW], ACT.Silu),
                     reads=[PSB[0]], writes=[hb])
                s.op("V", lambda e, hsl=hsl: e.tensor_tensor(hsl, hsl, PS[1][:, 0:FW], ALU.mult),
                     reads=[PSB[1], hb], writes=[hb])
                if zs is not None:
                    for _ in range(4):
                        zs()
                tick(1)

            def hquant_gen(ic):
                p_ = par(ic)
                p3 = ic % 3
                hb = B(f"hbuf{p_}_lo")
                hsl = HBUF[:, p_ * FW:(p_ + 1) * FW]
                hq = HQT[:, p3 * FW:(p3 + 1) * FW]
                qb = B("quantpool")
                axb, sgb = B("axt"), B("sgn")
                hqb = B(f"hqt{p3}")
                for i in range(FT):
                    s.op("T", lambda e, i=i: e.matmul(
                        PS[2][:, i * 128:(i + 1) * 128], hsl[:, i * 128:(i + 1) * 128],
                        IDENT[:, :], is_transpose=True, start=(i == 0), stop=(i == FT - 1)),
                        reads=[hb, onceb], writes=[PSB[2]], accum=(i > 0))
                    yield
                s.op("S", lambda e: e.copy(hq, PS[2][:, 0:FW]),
                     reads=[PSB[2]], writes=[hqb])
                yield
                s.op("S", lambda e: e.sign(SGN[:, 512:512 + FW], hsl), reads=[hb], writes=[sgb])
                yield "hfree"
                s.op("S", lambda e: e.activation(AXT[:, 0:FW], hq, ACT.Abs),
                     reads=[hqb], writes=[axb])
                yield
                s.op("V", lambda e: e.tensor_reduce(
                    QSC[:, 0:NGR], AXT[:, 0:FW].rearrange("p (g e) -> p g e", e=16),
                    AX.X, ALU.max), reads=[axb], writes=[qb])
                yield
                c6 = 0.16666667163372039794921875
                s.op("V", lambda e: e.tensor_single_scalar(QS2[:, 0:NGR], QSC[:, 0:NGR], c6, ALU.mult),
                     reads=[qb], writes=[qb])
                s.op("V", lambda e: e.tensor_single_scalar(QS2[:, 32:32 + NGR], QS2[:, 0:NGR], 6.0, ALU.mult),
                     reads=[qb], writes=[qb])
                yield
                s.op("V", lambda e: e.tensor_tensor(QS2[:, 32:32 + NGR], QSC[:, 0:NGR], QS2[:, 32:32 + NGR], ALU.subtract),
                     reads=[qb], writes=[qb])
                s.op("V", lambda e: e.tensor_single_scalar(QS2[:, 32:32 + NGR], QS2[:, 32:32 + NGR], c6, ALU.mult),
                     reads=[qb], writes=[qb])
                yield
                s.op("V", lambda e: e.tensor_tensor(QSC[:, 0:NGR], QS2[:, 0:NGR], QS2[:, 32:32 + NGR], ALU.add),
                     reads=[qb], writes=[qb])
                s.op("V", lambda e: e.tensor_single_scalar(QSC[:, 0:NGR], QSC[:, 0:NGR], 1e-12, ALU.max),
                     reads=[qb], writes=[qb])
                yield
                s.op("V", lambda e: e.reciprocal(QS2[:, 0:NGR], QSC[:, 0:NGR]),
                     reads=[qb], writes=[qb])
                s.op("V", lambda e: e.tensor_single_scalar(QSC[:, 0:NGR], QSC[:, 0:NGR], 0.5, ALU.mult),
                     reads=[qb], writes=[qb])
                yield
                rb = QS2[:, 0:NGR].unsqueeze(2).broadcast_to([128, NGR, 16])
                s.op("G", lambda e: e.tensor_tensor(
                    AXT[:, 0:FW].rearrange("p (g e) -> p g e", e=16),
                    AXT[:, 0:FW].rearrange("p (g e) -> p g e", e=16), rb, ALU.mult),
                    reads=[axb, qb], writes=[axb])
                yield
                u = AXT[:, 0:FW]
                v = SGN[:, 0:FW]
                s.op("V", lambda e: e.tensor_scalar(v, u, QTH[6], 2.0, ALU.is_gt, ALU.mult),
                     reads=[axb], writes=[sgb])
                yield
                for th in (QTH[5], QTH[4]):
                    s.op("V", lambda e, th=th: e.scalar_tensor_tensor(
                        v, u, th, v, ALU.is_gt, ALU.add), reads=[axb, sgb], writes=[sgb])
                    yield
                s.op("V", lambda e: e.tensor_single_scalar(v, v, 2.0, ALU.mult),
                     reads=[sgb], writes=[sgb])
                yield
                for th in (QTH[3], QTH[2], QTH[1], QTH[0]):
                    s.op("V", lambda e, th=th: e.scalar_tensor_tensor(
                        v, u, th, v, ALU.is_gt, ALU.add), reads=[axb, sgb], writes=[sgb])
                    yield
                sb_ = QSC[:, 0:NGR].unsqueeze(2).broadcast_to([128, NGR, 16])
                s.op("G", lambda e: e.tensor_tensor(
                    v.rearrange("p (g e) -> p g e", e=16),
                    v.rearrange("p (g e) -> p g e", e=16), sb_, ALU.mult),
                    reads=[qb, sgb], writes=[sgb])
                yield
                for i in range(FT):
                    s.op("T", lambda e, i=i: e.matmul(
                        PS[2][:, i * 128:(i + 1) * 128], v[:, i * 128:(i + 1) * 128],
                        IDENT[:, :], is_transpose=True, start=(i == 0), stop=(i == FT - 1)),
                        reads=[sgb, onceb], writes=[PSB[2]], accum=(i > 0))
                    yield
                s.op("S", lambda e, hq=hq: e.copy(hq, PS[2][:, 0:FW]),
                     reads=[PSB[2]], writes=[hqb])
                yield
                s.op("G", lambda e, hq=hq: e.tensor_tensor(hq, hq, SGN[:, 512:512 + FW], ALU.mult),
                     reads=[sgb, hqb], writes=[hqb])

            def emit_tail_pair(j0, j1):
                js = [j0] + ([j1] if j1 is not None else [])
                for i in range(FT):
                    c = t0 + i
                    for k, j_ in enumerate(js):
                        p_ = par(j_)
                        p3 = j_ % 3
                        dn = WSTR[:, 4096 + p_ * 1024: 4096 + (p_ + 1) * 1024]
                        for h in range(2):
                            pb = 3 + h
                            s.op("T", lambda e, pb=pb, i=i, h=h, dn=dn, p3=p3,
                                 st=(k == 0), sp=(k == len(js) - 1): e.matmul(
                                PS[pb][:, 0:512], HQT[:, p3 * FW + i * 128: p3 * FW + (i + 1) * 128],
                                dn[:, h * 512:(h + 1) * 512], start=st, stop=sp),
                                reads=[B(f"hqt{p3}"), dnslot[p_]], writes=[PSB[pb]],
                                accum=(k > 0))
                    for h in range(2):
                        pb = 3 + h
                        s.op("V", lambda e, pb=pb, h=h, c=c: e.tensor_tensor(
                            Y1[:, c * D + h * 512: c * D + (h + 1) * 512],
                            Y1[:, c * D + h * 512: c * D + (h + 1) * 512],
                            PS[pb][:, 0:512], ALU.add),
                            reads=[PSB[pb], y1b[c]], writes=[y1b[c]])
                    tick(1)

            def drain(gen):
                if gen is not None:
                    for _ in gen:
                        pass



            pre0 = gph > 0
            if not pre0:
                emit_load_gu(0)
            emit_load_dn(0)
            emit_load_dn(1)
            qgs = []
            pac = [None, False]

            def zipstep():
                while qgs:
                    r = next(qgs[0], "END")
                    if r == "END":
                        qgs.pop(0)
                        continue
                    return

            def pa_step():
                if pac[0] is None or pac[1]:
                    return
                r = next(pac[0], "END")
                if r == "END":
                    pac[0] = None
                elif r == "TGREADY":
                    pac[1] = True

            def zs2():
                zipstep()
                pa_step()

            for ic in range(NIC):
                if not (ic == 0 and pre0):
                    emit_gateup(ic, zs2)
                if ic + 1 < NIC:
                    emit_load_gu(ic + 1)
                if ic == NIC - 1 and nxt_fn is not None:
                    emit_load_gu(0, pi=NIC)
                if ic >= 1:
                    while qgs:
                        drain(qgs.pop(0))
                    if ic % 2 == 0:
                        emit_tail_pair(ic - 2, ic - 1)
                        emit_load_dn(ic)
                        emit_load_dn(ic + 1)
                if ic == NIC - 3 and nxt_fn is not None:
                    pac[0] = nxt_fn()
                qgs.append(hquant_gen(ic))
                for _ in range(6):
                    zs2()
            pac[1] = False
            while qgs or pac[0] is not None:
                if qgs:
                    zipstep()
                pa_step()
            if nxt_fn is not None:
                emit_gateup(0, pi=NIC)
            emit_tail_pair(NIC - 2, NIC - 1)

            for i in range(FT):
                c = t0 + i
                s.dma("SQ", f"st{c}", lambda e, b=b, c=c: e.dma_start(
                    out=y_out[b, c * 128:(c + 1) * 128, :],
                    in_=Y1[:, c * D:(c + 1) * D]), reads=[y1b[c]])

        AQ = [[0, c, attn_chunk_gen(0, c)] for c in range(NCH)]

        def advance(n):
            done = 0
            while AQ and done < n:
                try:
                    next(AQ[0][2])
                    done += 1
                except StopIteration:
                    AQ.pop(0)
            return done

        def drain_until(b, cmax):
            while AQ and (AQ[0][0] < b or (AQ[0][0] == b and AQ[0][1] <= cmax)):
                advance(10 ** 6)

        wq = []
        for k, u in enumerate(wprep_units):
            if u[0] == "gu":
                wq.append(wprep_gateup_gen(k, u[1], u[2], u[3]))
            else:
                wq.append(wprep_down_gen(k, u[1], u[2], u[3]))
        act = []
        while wq or act:
            while len(act) < 2 and wq:
                act.append(wq.pop(0))
            for g in list(act):
                if next(g, "END") == "END":
                    act.remove(g)
            advance(1)

        def mk_nxt(nb, nph):
            def f():
                drain_until(nb, (nph + 1) * FT - 1)
                return phaseA_gen(nb, nph)
            return f

        drain_until(0, FT - 1)
        for _ in phaseA_gen(0, 0):
            advance(1)

        for b in range(NB):
            for ph in range(NPH):
                drain_until(b, (ph + 1) * FT - 1)
                if ph + 1 < NPH:
                    nxt = mk_nxt(b, ph + 1)
                elif b + 1 < NB:
                    nxt = mk_nxt(b + 1, 0)
                else:
                    nxt = None
                ffn_phase(b, ph, nxt, AQ, lambda n=1: advance(n))
                if b + 1 < NB:
                    AQ.extend([[b + 1, c, attn_chunk_gen(b + 1, c)]
                               for c in range(ph * FT, (ph + 1) * FT)])
        advance(10 ** 9)

        s.raw_wait_all("S", "wps")
        for c in range(NCH):
            s.raw_wait_all("S", f"st{c}")

    with nc.Block() as block:
        s.emit(block)
    return nc


def host_inputs(xs, w, ft=4):
    return {
        "x": np.ascontiguousarray(xs),
        "wcat": np.ascontiguousarray(
            np.concatenate([w["Wq"], w["Wk"], w["Wv"], w["Wg"]], axis=1)),
        "wab": np.ascontiguousarray(np.concatenate([w["Wa"], w["Wb"]], axis=1)),
        "wo": np.ascontiguousarray(w["Wo"]),
        "bqkvg": np.ascontiguousarray(np.tile(
            np.concatenate([w["bq"], w["bk"], w["bv"], w["bg"]])[None, :], (128, 1))),
        "bab": np.ascontiguousarray(np.tile(
            np.concatenate([w["ba"], w["bb"]])[None, :], (128, 1))),
        "bo": np.ascontiguousarray(np.tile(w["bo"][None, :], (128, 1))),
        "wgate": np.ascontiguousarray(w["w_gate"]),
        "wup": np.ascontiguousarray(w["w_up"]),
        "wdown": np.ascontiguousarray(w["w_down"]),
        "ident": np.eye(128, dtype=np.float32),
        "mask_gt": (np.arange(128)[None, :] > np.arange(128)[:, None]).astype(np.float32),
        "mask_ge": (np.arange(128)[None, :] >= np.arange(128)[:, None]).astype(np.float32),
        "ones1": np.ones((1, 128), dtype=np.float32),
    }


def pack_weights(L0_attn_Wq, L0_attn_bq, L0_attn_Wk, L0_attn_bk, L0_attn_Wv,
                 L0_attn_bv, L0_attn_Wa, L0_attn_ba, L0_attn_Wb, L0_attn_bb,
                 L0_attn_Wg, L0_attn_bg, L0_attn_Wo, L0_attn_bo,
                 L0_ffn_w_gate, L0_ffn_w_up, L0_ffn_w_down):
    f = lambda a: np.asarray(a, np.float32)
    return {
        "Wq": f(L0_attn_Wq), "bq": f(L0_attn_bq), "Wk": f(L0_attn_Wk), "bk": f(L0_attn_bk),
        "Wv": f(L0_attn_Wv), "bv": f(L0_attn_bv), "Wa": f(L0_attn_Wa), "ba": f(L0_attn_ba),
        "Wb": f(L0_attn_Wb), "bb": f(L0_attn_bb), "Wg": f(L0_attn_Wg), "bg": f(L0_attn_bg),
        "Wo": f(L0_attn_Wo), "bo": f(L0_attn_bo),
        "w_gate": f(L0_ffn_w_gate), "w_up": f(L0_ffn_w_up), "w_down": f(L0_ffn_w_down),
    }


def kernel(L0_attn_x, **kw):
    x = np.asarray(L0_attn_x, dtype=np.float32)
    Bfull, T, _ = x.shape
    n_cores = 8
    nb = Bfull // n_cores
    w = pack_weights(**kw)
    nc = build_program(NB=nb, T=T, FT=4)
    in_maps = [host_inputs(x[c * nb:(c + 1) * nb], w, 4) for c in range(n_cores)]
    res = run_bass_kernel_spmd(nc, in_maps, core_ids=list(range(n_cores)))
    return np.concatenate([res.results[c]["y"] for c in range(n_cores)], axis=0)


# version 33
# speedup vs baseline: 1.0390x; 1.0390x over previous
import numpy as np
import concourse.bass as bass
import concourse.mybir as mybir
from concourse.bass_utils import run_bass_kernel_spmd

F32 = mybir.dt.float32
ALU = mybir.AluOpType
ACT = mybir.ActivationFunctionType
AX = mybir.AxisListType

D = 1024
C = 128
HID = 2816
NKC = D // 128
NIC = HID // 128
QTH = [0.25, 0.75, 1.25, 1.75, 2.5, 3.5, 5.0]


class Buf:
    __slots__ = ("name", "writer", "readers", "unread", "is_psum")

    def __init__(self, name, is_psum=False):
        self.name = name
        self.writer = None
        self.readers = []
        self.unread = False
        self.is_psum = is_psum


class Sched:

    PHYS = {"T": "T", "V": "V", "S": "S", "G": "G", "LQ": "LQ", "SQ": "S"}

    def __init__(self, nc):
        self.nc = nc
        self.sems = {}
        self.count = {}
        self.dma_key = set()
        self.hw = {}
        self.prog = {k: [] for k in ("T", "V", "S", "G", "LQ")}
        self.labels = {}

    def add_engine(self, key):
        self.sems[key] = self.nc.alloc_semaphore(name=f"sem_{key}")
        self.count[key] = 0

    def add_dma_sem(self, key):
        self.sems[key] = self.nc.alloc_semaphore(name=f"sem_{key}")
        self.count[key] = 0
        self.dma_key.add(key)

    def _wait(self, phys, sem_key, val):
        if sem_key in self.dma_key:
            val = self.count[sem_key]
        if val <= 0:
            return
        hwk = (phys, sem_key)
        if self.hw.get(hwk, 0) >= val:
            return
        sem = self.sems[sem_key]
        self.prog[phys].append(lambda e, sem=sem, val=val: e.wait_ge(sem, val))
        self.hw[hwk] = val

    def _pre(self, phys, reads, writes):
        need = {}
        for b in reads:
            if b.writer is not None:
                k, v = b.writer
                need[k] = max(need.get(k, 0), v)
        for b in writes:
            if b.writer is not None:
                k, v = b.writer
                need[k] = max(need.get(k, 0), v)
            for k, v in b.readers:
                need[k] = max(need.get(k, 0), v)
        for k in sorted(need):
            self._wait(phys, k, need[k])

    def op(self, ekey, f, reads=(), writes=(), accum=False):
        import sys
        fr = sys._getframe(1)
        label = f"{fr.f_code.co_name}:{fr.f_lineno}"
        for b in writes:
            if b.is_psum and b.unread and not accum:
                raise RuntimeError(
                    f"psum bank {b.name}: new group before evac "
                    f"(last writer {b.writer} = "
                    f"{self.labels.get(b.writer, '?')}, here {label})")
        phys = self.PHYS[ekey]
        self._pre(phys, reads, writes)
        self.count[ekey] += 1
        sem = self.sems[ekey]
        self.prog[phys].append(lambda e, f=f, sem=sem: f(e).then_inc(sem, 1))
        me = (ekey, self.count[ekey])
        self.labels[me] = label
        for b in writes:
            b.writer = me
            b.readers = []
            b.unread = True
        for b in reads:
            b.readers.append(me)
            b.unread = False

    def dma(self, qkey, slot_key, f, reads=(), writes=()):
        phys = self.PHYS[qkey]
        self._pre(phys, reads, writes)
        self.count[slot_key] += 16
        sem = self.sems[slot_key]
        self.prog[phys].append(lambda e, f=f, sem=sem: f(e).then_inc(sem, 16))
        me = (slot_key, self.count[slot_key])
        for b in writes:
            b.writer = me
            b.readers = []
            b.unread = True
        for b in reads:
            b.readers.append(me)
            b.unread = False

    def raw_wait_all(self, ekey, sem_key):
        self._wait(self.PHYS[ekey], sem_key, self.count[sem_key])

    def emit(self, block):
        def runner(phys):
            def run(eng):
                for item in self.prog[phys]:
                    item(eng)
            return run
        block.tensor(runner("T"))
        block.vector(runner("V"))
        block.scalar(runner("S"))
        block.gpsimd(runner("G"))
        block.sync(runner("LQ"))


def build_program(NB, T, FT, debug_bc=None):
    NCH = T // 128
    assert NCH % FT == 0
    NPH = NCH // FT
    FW = FT * 128
    nc = bass.Bass("TRN2", target_bir_lowering=False)

    x_in = nc.dram_tensor("x", [NB, T, D], F32, kind="ExternalInput").ap()
    y_out = nc.dram_tensor("y", [NB, T, D], F32, kind="ExternalOutput").ap()
    wcat_in = nc.dram_tensor("wcat", [D, 512], F32, kind="ExternalInput").ap()
    wab_in = nc.dram_tensor("wab", [D, 2], F32, kind="ExternalInput").ap()
    wo_in = nc.dram_tensor("wo", [C, D], F32, kind="ExternalInput").ap()
    bqkvg_in = nc.dram_tensor("bqkvg", [128, 512], F32, kind="ExternalInput").ap()
    bab_in = nc.dram_tensor("bab", [128, 2], F32, kind="ExternalInput").ap()
    bo_in = nc.dram_tensor("bo", [128, D], F32, kind="ExternalInput").ap()
    wg_in = nc.dram_tensor("wgate", [HID, D], F32, kind="ExternalInput").ap()
    wu_in = nc.dram_tensor("wup", [HID, D], F32, kind="ExternalInput").ap()
    wd_in = nc.dram_tensor("wdown", [D, HID], F32, kind="ExternalInput").ap()
    ident_in = nc.dram_tensor("ident", [128, 128], F32, kind="ExternalInput").ap()
    mgt_in = nc.dram_tensor("mask_gt", [128, 128], F32, kind="ExternalInput").ap()
    mge_in = nc.dram_tensor("mask_ge", [128, 128], F32, kind="ExternalInput").ap()
    ones1_in = nc.dram_tensor("ones1", [1, 128], F32, kind="ExternalInput").ap()

    dbg = nc.dram_tensor("dbg", [8, 128, 128], F32, kind="ExternalOutput").ap() if debug_bc else None
    wgus = nc.dram_tensor("wgus", [NIC, NKC, 128, 256], F32, kind="Internal").ap()
    wdqs = nc.dram_tensor("wdqs", [NIC, 128, D], F32, kind="Internal").ap()

    def sbt(name, shape):
        return nc.alloc_sbuf_tensor(name, shape, F32).ap()

    WCAT = sbt("WCAT", [128, NKC * 512])
    WAB = sbt("WAB", [128, NKC * 2])
    WO = sbt("WO", [128, D])
    BQKVG = sbt("BQKVG", [128, 512])
    BAB = sbt("BAB", [128, 2])
    BO = sbt("BO", [128, D])
    IDENT = sbt("IDENT", [128, 128])
    MGT = sbt("MGT", [128, 128])
    MGE = sbt("MGE", [128, 128])
    NGR = FW // 16
    ONES1 = sbt("ONES1", [1, 128])
    ZEROS = sbt("ZEROS", [128, 128])

    Y1 = sbt("Y1", [128, NCH * D])
    XQT = sbt("XQT", [128, NKC * FW])
    WSTR = sbt("WSTR", [128, 6144])
    STT = sbt("STATE_T", [128, NB * C])
    SSS = sbt("STATE_S", [128, NB * C])

    LNS = sbt("LNS", [128, D])
    LNT = sbt("LNT", [128, D])
    QKVG = sbt("QKVG", [128, 512])
    GT_ = sbt("GTILE", [128, 128])
    KTL = sbt("KTILE", [128, 128])
    KTQT = sbt("KTQT", [128, 256])
    ABP = sbt("ABP", [128, 2])
    ABROW = sbt("ABROW", [2, 128])
    DROW = sbt("DROW", [1, 128])
    GAM = sbt("GAM", [128, 128])
    GAMA = sbt("GAMA", [128, 128])
    DCOL = sbt("DCOL", [128, 1])
    DLC = sbt("DLC", [128, 1])
    NBD = sbt("NBD", [128, 1])
    NBC = sbt("NBC", [128, 1])
    XW = sbt("XW", [128, 128])
    XWT = sbt("XWT", [128, 128])
    XP = [sbt(f"XP{i}", [128, 128]) for i in range(2)]
    XPT = [sbt(f"XPT{i}", [128, 128]) for i in range(2)]
    YV = sbt("YV", [128, 128])
    VT = sbt("VT", [128, 128])
    WTT = sbt("WTT", [128, 128])
    OG = sbt("OG", [128, 128])
    OT = sbt("OT", [128, 128])
    TMP1 = sbt("TMP1", [128, 128])
    TMP2 = sbt("TMP2", [128, 128])
    SMALL = sbt("SMALLP", [128, 16])

    HBUF = sbt("HBUF", [128, 2 * FW])
    HQT = sbt("HQT", [128, 3 * FW])
    QSC = sbt("QSC", [128, 128])
    QS2 = sbt("QS2", [128, 128])
    AXT = sbt("AXT", [128, 1024])
    SGN = sbt("SGN", [128, 1024])
    WPQ = sbt("WPQ", [128, 1024])
    PAX = sbt("PAX", [128, 1024])
    PASGN = sbt("PASGN", [128, 1024])
    PASC = sbt("PASC", [128, 192])
    WPQ2 = sbt("WPQ2", [128, 1024])
    PAX2 = sbt("PAX2", [128, 1024])
    PASGN2 = sbt("PASGN2", [128, 1024])
    PASC2 = sbt("PASC2", [128, 192])
    WSTG = sbt("WSTG", [128, 512])

    PS = [nc.alloc_psum_tensor(f"ps{i}", [128, 512], F32).ap() for i in range(8)]

    s = Sched(nc)
    bufs = {}

    def B(name):
        if name not in bufs:
            bufs[name] = Buf(name, is_psum=name.startswith("ps"))
        return bufs[name]

    PSB = [B(f"ps{i}") for i in range(8)]
    xslot = [B("xslot0"), B("xslot1")]
    wslot = [B("wslot0"), B("wslot1")]
    dnslot = [B("dnslot0"), B("dnslot1")]
    y1b = [B(f"y1_{i}") for i in range(NCH)]
    xqtb = [B(f"xqt_{i}") for i in range(FT)]
    stb = [B(f"state_t{i}") for i in range(NB)]
    ssb = [B(f"state_s{i}") for i in range(NB)]
    onceb = B("once")
    wgusb = B("wgus_dram")
    wdqsb = B("wdqs_dram")

    if True:
        for k in ("T", "V", "S", "G", "LQ"):
            s.add_engine(k)
        for k in ("xs0", "xs1", "ws0", "ws1", "dns0", "dns1", "once", "wps"):
            s.add_dma_sem(k)
        for c in range(NCH):
            s.add_dma_sem(f"st{c}")
        xsem = ["xs0", "xs1"]
        wsem = ["ws0", "ws1"]

        def once_load(dst, src):
            s.dma("LQ", "once", lambda e: e.dma_start(out=dst, in_=src),
                  writes=[onceb])

        once_load(WCAT[:, :].rearrange("p (kc n) -> p kc n", n=512),
                  wcat_in.rearrange("(kc p) n -> p kc n", p=128))
        once_load(WAB[:, :].rearrange("p (kc n) -> p kc n", n=2),
                  wab_in.rearrange("(kc p) n -> p kc n", p=128))
        once_load(WO[:, :], wo_in[:, :])
        once_load(BQKVG[:, :], bqkvg_in[:, :])
        once_load(BAB[:, :], bab_in[:, :])
        once_load(BO[:, :], bo_in[:, :])
        once_load(IDENT[:, :], ident_in[:, :])
        once_load(MGT[:, :], mgt_in[:, :])
        once_load(MGE[:, :], mge_in[:, :])
        once_load(ONES1[:, :], ones1_in[:, :])
        s.op("V", lambda e: e.memset(ZEROS[:, :], 0.0), writes=[onceb])
        for b in range(NB):
            s.op("V", lambda e, b=b: e.memset(STT[:, b * C:(b + 1) * C], 0.0),
                 writes=[stb[b]])
            s.op("V", lambda e, b=b: e.memset(SSS[:, b * C:(b + 1) * C], 0.0),
                 writes=[ssb[b]])

        def tgroup(srcs, bank, dsts):
            n = len(srcs)
            for i, (ap, rb) in enumerate(srcs):
                s.op("T", lambda e, i=i, ap=ap, last=(i == n - 1): e.matmul(
                    PS[bank][:, i * 128:(i + 1) * 128], ap, IDENT[:, :],
                    is_transpose=True, start=(i == 0), stop=last),
                    reads=list(rb) + [onceb], writes=[PSB[bank]],
                    accum=(i > 0))
            for (ap, wb, ek, lo, hi) in dsts:
                if ek == "S":
                    s.op("S", lambda e, ap=ap, lo=lo, hi=hi: e.copy(
                        ap, PS[bank][:, lo:hi]), reads=[PSB[bank]], writes=list(wb))
                else:
                    s.op(ek, lambda e, ap=ap, lo=lo, hi=hi: e.tensor_copy(
                        ap, PS[bank][:, lo:hi]), reads=[PSB[bank]], writes=list(wb))

        def layernorm(x_ap, x_bufs, out_ap, out_bufs, sq_ap=None, sq_buf=None,
                      small0=0):
            m = SMALL[:, small0:small0 + 1]
            vv = SMALL[:, small0 + 1:small0 + 2]
            r0 = SMALL[:, small0 + 2:small0 + 3]
            t1 = SMALL[:, small0 + 3:small0 + 4]
            smb = B(f"smallpool{small0}")
            if sq_ap is None:
                sq_ap, sq_buf = LNT[:, :], B("lnt")
            ctb, sqb = out_bufs[0], sq_buf
            s.op("V", lambda e: e.reduce_sum(m, x_ap, AX.X), reads=list(x_bufs), writes=[smb])
            s.op("V", lambda e: e.tensor_single_scalar(m, m, 1.0 / D, ALU.mult),
                 reads=[smb], writes=[smb])
            s.op("V", lambda e: e.tensor_single_scalar(out_ap, x_ap, m, ALU.subtract),
                 reads=list(x_bufs) + [smb], writes=[ctb])
            s.op("S", lambda e: e.activation(sq_ap, out_ap, ACT.Square, accum_out=vv),
                 reads=[ctb], writes=[sqb, smb])
            s.op("V", lambda e: e.tensor_scalar(vv, vv, 1.0 / D, 1e-5, ALU.mult, ALU.add),
                 reads=[smb], writes=[smb])
            s.op("S", lambda e: e.activation(t1, vv, ACT.Sqrt), reads=[smb], writes=[smb])
            s.op("V", lambda e: e.reciprocal(r0, t1), reads=[smb], writes=[smb])
            s.op("V", lambda e: e.tensor_single_scalar(out_ap, out_ap, r0, ALU.mult),
                 reads=[smb] + list(out_bufs), writes=list(out_bufs))

        def quant_gen(x_ap, x_bufs, out_ap, out_bufs, width):
            assert width <= 1024 and width % 32 == 0
            gb = width // 16
            qb = B("quantpool")
            axb, sgb = B("axt"), B("sgn")
            ob = out_bufs[0]
            s.op("V", lambda e: e.tensor_reduce(
                QSC[:, 0:gb], x_ap.rearrange("p (g e) -> p g e", e=16),
                AX.X, ALU.max, apply_absolute_value=True),
                reads=list(x_bufs), writes=[qb])
            yield
            c6 = 0.16666667163372039794921875
            s.op("V", lambda e: e.tensor_single_scalar(QS2[:, 0:gb], QSC[:, 0:gb], c6, ALU.mult),
                 reads=[qb], writes=[qb])
            s.op("V", lambda e: e.tensor_single_scalar(QS2[:, 64:64 + gb], QS2[:, 0:gb], 6.0, ALU.mult),
                 reads=[qb], writes=[qb])
            yield
            s.op("V", lambda e: e.tensor_tensor(QS2[:, 64:64 + gb], QSC[:, 0:gb], QS2[:, 64:64 + gb], ALU.subtract),
                 reads=[qb], writes=[qb])
            s.op("V", lambda e: e.tensor_single_scalar(QS2[:, 64:64 + gb], QS2[:, 64:64 + gb], c6, ALU.mult),
                 reads=[qb], writes=[qb])
            yield
            s.op("V", lambda e: e.tensor_tensor(QSC[:, 0:gb], QS2[:, 0:gb], QS2[:, 64:64 + gb], ALU.add),
                 reads=[qb], writes=[qb])
            s.op("V", lambda e: e.tensor_single_scalar(QSC[:, 0:gb], QSC[:, 0:gb], 1e-12, ALU.max),
                 reads=[qb], writes=[qb])
            yield
            s.op("V", lambda e: e.reciprocal(QS2[:, 0:gb], QSC[:, 0:gb]),
                 reads=[qb], writes=[qb])
            s.op("V", lambda e: e.tensor_single_scalar(QSC[:, 0:gb], QSC[:, 0:gb], 0.5, ALU.mult),
                 reads=[qb], writes=[qb])
            yield
            s.op("S", lambda e: e.activation(AXT[:, 0:width], x_ap, ACT.Abs),
                 reads=list(x_bufs), writes=[axb])
            yield
            s.op("S", lambda e: e.sign(SGN[:, 0:width], x_ap),
                 reads=list(x_bufs), writes=[sgb])
            yield
            rb = QS2[:, 0:gb].unsqueeze(2).broadcast_to([128, gb, 16])
            s.op("G", lambda e: e.tensor_tensor(
                AXT[:, 0:width].rearrange("p (g e) -> p g e", e=16),
                AXT[:, 0:width].rearrange("p (g e) -> p g e", e=16), rb, ALU.mult),
                reads=[axb, qb], writes=[axb])
            yield
            u = AXT[:, 0:width]
            v = out_ap
            s.op("V", lambda e: e.tensor_scalar(v, u, QTH[6], 2.0, ALU.is_gt, ALU.mult),
                 reads=[axb], writes=[ob])
            yield
            for th in (QTH[5], QTH[4]):
                s.op("V", lambda e, th=th: e.scalar_tensor_tensor(
                    v, u, th, v, ALU.is_gt, ALU.add), reads=[axb, ob], writes=[ob])
                yield
            s.op("V", lambda e: e.tensor_single_scalar(v, v, 2.0, ALU.mult),
                 reads=[ob], writes=[ob])
            yield
            for th in (QTH[3], QTH[2], QTH[1], QTH[0]):
                s.op("V", lambda e, th=th: e.scalar_tensor_tensor(
                    v, u, th, v, ALU.is_gt, ALU.add), reads=[axb, ob], writes=[ob])
                yield
            sb_ = QSC[:, 0:gb].unsqueeze(2).broadcast_to([128, gb, 16])
            s.op("G", lambda e: e.tensor_tensor(
                out_ap.rearrange("p (g e) -> p g e", e=16),
                out_ap.rearrange("p (g e) -> p g e", e=16), sb_, ALU.mult),
                reads=[qb, ob], writes=[ob])
            yield
            s.op("G", lambda e: e.tensor_tensor(out_ap, out_ap, SGN[:, 0:width], ALU.mult),
                 reads=[sgb, ob], writes=[ob])

        def qchain_gen(x_ap, x_bufs, out_ap, out_bufs, width, u_ap, u_buf,
                       sg_ap, sg_buf, sc0, pool, q_aps=None):
            gb = width // 16
            u_ap = u_ap[:, 0:width]
            sg_ap = sg_ap[:, 0:width]
            qb = B(pool)
            ob = out_bufs[0]
            if q_aps is None:
                Q0 = QSC[:, sc0:sc0 + gb]
                Q1 = QS2[:, sc0:sc0 + gb]
                Q2 = QS2[:, sc0 + 32:sc0 + 32 + gb]
            else:
                Q0, Q1, Q2 = q_aps
            s.op("V", lambda e: e.tensor_reduce(
                Q0, x_ap.rearrange("p (g e) -> p g e", e=16),
                AX.X, ALU.max, apply_absolute_value=True),
                reads=list(x_bufs), writes=[qb])
            yield
            c6 = 0.16666667163372039794921875
            s.op("V", lambda e: e.tensor_single_scalar(Q1, Q0, c6, ALU.mult),
                 reads=[qb], writes=[qb])
            s.op("S", lambda e: e.activation(u_ap, x_ap, ACT.Abs),
                 reads=list(x_bufs), writes=[u_buf])
            yield
            s.op("S", lambda e: e.sign(sg_ap, x_ap), reads=list(x_bufs), writes=[sg_buf])
            s.op("V", lambda e: e.tensor_single_scalar(Q2, Q1, 6.0, ALU.mult),
                 reads=[qb], writes=[qb])
            yield
            s.op("V", lambda e: e.tensor_tensor(Q2, Q0, Q2, ALU.subtract),
                 reads=[qb], writes=[qb])
            s.op("V", lambda e: e.tensor_single_scalar(Q2, Q2, c6, ALU.mult),
                 reads=[qb], writes=[qb])
            yield
            s.op("V", lambda e: e.tensor_tensor(Q0, Q1, Q2, ALU.add),
                 reads=[qb], writes=[qb])
            s.op("V", lambda e: e.tensor_single_scalar(Q0, Q0, 1e-12, ALU.max),
                 reads=[qb], writes=[qb])
            yield
            s.op("V", lambda e: e.reciprocal(Q1, Q0), reads=[qb], writes=[qb])
            s.op("V", lambda e: e.tensor_single_scalar(Q0, Q0, 0.5, ALU.mult),
                 reads=[qb], writes=[qb])
            yield
            rb = Q1.unsqueeze(2).broadcast_to([128, gb, 16])
            s.op("G", lambda e: e.tensor_tensor(
                u_ap.rearrange("p (g e) -> p g e", e=16),
                u_ap.rearrange("p (g e) -> p g e", e=16), rb, ALU.mult),
                reads=[u_buf, qb], writes=[u_buf])
            yield
            s.op("V", lambda e: e.tensor_scalar(out_ap, u_ap, QTH[6], 2.0, ALU.is_gt, ALU.mult),
                 reads=[u_buf], writes=[ob])
            yield
            for th in (QTH[5], QTH[4]):
                s.op("V", lambda e, th=th: e.scalar_tensor_tensor(
                    out_ap, u_ap, th, out_ap, ALU.is_gt, ALU.add), reads=[u_buf, ob], writes=[ob])
                yield
            s.op("V", lambda e: e.tensor_single_scalar(out_ap, out_ap, 2.0, ALU.mult),
                 reads=[ob], writes=[ob])
            yield
            for th in (QTH[3], QTH[2], QTH[1], QTH[0]):
                s.op("V", lambda e, th=th: e.scalar_tensor_tensor(
                    out_ap, u_ap, th, out_ap, ALU.is_gt, ALU.add), reads=[u_buf, ob], writes=[ob])
                yield
            sb_ = Q0.unsqueeze(2).broadcast_to([128, gb, 16])
            s.op("G", lambda e: e.tensor_tensor(
                out_ap.rearrange("p (g e) -> p g e", e=16),
                out_ap.rearrange("p (g e) -> p g e", e=16), sb_, ALU.mult),
                reads=[qb, ob], writes=list(out_bufs))
            yield
            s.op("G", lambda e: e.tensor_tensor(out_ap, out_ap, sg_ap, ALU.mult),
                 reads=[sg_buf, ob], writes=list(out_bufs))

        def quant(x_ap, x_bufs, out_ap, out_bufs, width):
            for _ in quant_gen(x_ap, x_bufs, out_ap, out_bufs, width):
                pass

        wpsets = [
            dict(ub="axtA", sgb="sgnA", pool="wqpoolA", bank=3,
                 u=AXT[:, 0:1024], sg=SGN[:, 0:1024],
                 q=((QSC, 0), (QS2, 0), (QS2, 64)),
                 stg=WSTG[:, 0:512], stgb="wstg"),
            dict(ub="pax", sgb="pasgn", pool="papool", bank=5,
                 u=PAX[:, 0:1024], sg=PASGN[:, 0:1024],
                 q=((PASC, 0), (PASC, 64), (PASC, 128)),
                 stg=WPQ[:, 0:512], stgb="wpq"),
        ]

        def wp_chain(st, sl, slb, pw):
            gb = pw // 16
            qaps = tuple(t[:, o:o + gb] for (t, o) in st["q"])
            return qchain_gen(sl, [slb], sl, [slb], pw,
                              st["u"], B(st["ub"]), st["sg"], B(st["sgb"]),
                              0, st["pool"], q_aps=qaps)

        def wprep_gateup_gen(k, src_row0, which_in, half):
            st = wpsets[k % 2]
            slot = k % 2
            sl = WSTR[:, slot * 2048: slot * 2048 + D]
            slb = wslot[slot]
            s.dma("LQ", wsem[slot], lambda e, sl=sl: e.dma_start(
                out=sl, in_=which_in[src_row0:src_row0 + 128, :]), writes=[slb])
            yield
            yield from wp_chain(st, sl, slb, D)
            yield
            nt = src_row0 // 128
            for g in range(2):
                tgroup([(sl[:, (g * 4 + i) * 128:(g * 4 + i + 1) * 128], [slb])
                        for i in range(4)], st["bank"],
                       [(st["stg"], [B(st["stgb"])], "S", 0, 512)])
                yield
                for i in range(4):
                    kc = g * 4 + i
                    s.dma("SQ", "wps", lambda e, kc=kc, i=i, nt=nt, st=st,
                          half=half: e.dma_start(
                        out=wgus[nt, kc, :, half * 128:(half + 1) * 128],
                        in_=st["stg"][:, i * 128:(i + 1) * 128]),
                        reads=[B(st["stgb"])], writes=[wgusb])
                yield

        def wprep_down_gen(k, nt, p0, pw):
            st = wpsets[k % 2]
            slot = k % 2
            sl = WSTR[:, slot * 2048: slot * 2048 + pw]
            slb = wslot[slot]
            s.dma("LQ", wsem[slot], lambda e, sl=sl, p0=p0, pw=pw: e.dma_start(
                out=sl, in_=wd_in[nt * 128:(nt + 1) * 128, p0:p0 + pw]),
                writes=[slb])
            yield
            yield from wp_chain(st, sl, slb, pw)
            yield
            nics = pw // 128
            for g in range((nics + 3) // 4):
                ii = list(range(g * 4, min(g * 4 + 4, nics)))
                tgroup([(sl[:, i * 128:(i + 1) * 128], [slb]) for i in ii],
                       st["bank"],
                       [(st["stg"][:, 0:len(ii) * 128], [B(st["stgb"])], "S",
                         0, len(ii) * 128)])
                yield
                for j, i in enumerate(ii):
                    ic = p0 // 128 + i
                    s.dma("SQ", "wps", lambda e, ic=ic, j=j, nt=nt,
                          st=st: e.dma_start(
                        out=wdqs[ic, :, nt * 128:(nt + 1) * 128],
                        in_=st["stg"][:, j * 128:(j + 1) * 128]),
                        reads=[B(st["stgb"])], writes=[wdqsb])
                yield

        wprep_units = []
        for nt in range(NIC):
            wprep_units.append(("gu", nt * 128, wg_in, 0))
            wprep_units.append(("gu", nt * 128, wu_in, 1))
        for nt in range(NKC):
            for (p0, pw) in ((0, 1024), (1024, 1024), (2048, 768)):
                wprep_units.append(("dn", nt, p0, pw))

        chunk_no = [0]

        def attn_chunk_gen(b, c):
            slot = chunk_no[0] % 2
            chunk_no[0] += 1
            xa = Y1[:, c * D:(c + 1) * D]
            xb = y1b[c]
            s.dma("LQ", xsem[slot], lambda e: e.dma_start(
                out=xa, in_=x_in[b, c * 128:(c + 1) * 128, :]), writes=[xb])

            lnb = B("lns")
            layernorm(xa, [xb], LNS[:, :], [lnb])
            yield

            for g in range(2):
                tgroup([(LNS[:, (g * 4 + i) * 128:(g * 4 + i + 1) * 128], [lnb])
                        for i in range(4)], 7,
                       [(LNT[:, g * 512:(g + 1) * 512], [B("lnt")], "S", 0, 512)])
            yield

            for kc in range(NKC):
                s.op("T", lambda e, kc=kc: e.matmul(
                    PS[6][:, 0:512], LNT[:, kc * 128:(kc + 1) * 128],
                    WCAT[:, kc * 512:(kc + 1) * 512],
                    start=(kc == 0), stop=(kc == NKC - 1)),
                    reads=[B("lnt"), onceb], writes=[PSB[6]], accum=(kc > 0))
                yield
            for kc in range(NKC):
                s.op("T", lambda e, kc=kc: e.matmul(
                    PS[7][:, 384:386], LNT[:, kc * 128:(kc + 1) * 128],
                    WAB[:, kc * 2:(kc + 1) * 2],
                    start=(kc == 0), stop=(kc == NKC - 1)),
                    reads=[B("lnt"), onceb], writes=[PSB[7]], accum=(kc > 0))
                yield

            s.op("V", lambda e: e.tensor_tensor(QKVG[:, :], PS[6][:, 0:512], BQKVG[:, :], ALU.add),
                 reads=[PSB[6], onceb], writes=[B("qkvg")])
            s.op("S", lambda e: e.activation(ABP[:, 0:1], PS[7][:, 384:385], ACT.Sigmoid,
                                             bias=BAB[:, 0:1]),
                 reads=[PSB[7], onceb], writes=[B("abp")])
            s.op("S", lambda e: e.activation(ABP[:, 1:2], PS[7][:, 385:386], ACT.Sigmoid,
                                             bias=BAB[:, 1:2]),
                 reads=[PSB[7], onceb], writes=[B("abp")])
            s.op("S", lambda e: e.activation(GT_[:, :], QKVG[:, 384:512], ACT.Sigmoid),
                 reads=[B("qkvg")], writes=[B("gtile")])
            yield

            smb = B("smallpool")
            s.op("S", lambda e: e.activation(TMP1[:, :], QKVG[:, 128:256], ACT.Square,
                                             accum_out=SMALL[:, 4:5]),
                 reads=[B("qkvg")], writes=[B("tmp1"), smb])
            s.op("V", lambda e: e.tensor_single_scalar(SMALL[:, 4:5], SMALL[:, 4:5], 1e-38, ALU.max),
                 reads=[smb], writes=[smb])
            s.op("S", lambda e: e.activation(SMALL[:, 5:6], SMALL[:, 4:5], ACT.Sqrt),
                 reads=[smb], writes=[smb])
            s.op("V", lambda e: e.reciprocal(SMALL[:, 6:7], SMALL[:, 5:6]),
                 reads=[smb], writes=[smb])
            s.op("V", lambda e: e.tensor_single_scalar(SMALL[:, 6:7], SMALL[:, 6:7], 1e12, ALU.min),
                 reads=[smb], writes=[smb])
            s.op("V", lambda e: e.tensor_single_scalar(KTL[:, :], QKVG[:, 128:256], SMALL[:, 6:7], ALU.mult),
                 reads=[B("qkvg"), smb], writes=[B("ktile")])

            tgroup([(KTL[:, :], [B("ktile")]), (QKVG[:, 0:128], [B("qkvg")])], 7,
                   [(KTQT[:, 0:256], [B("ktqt")], "S", 0, 256)])
            yield

            s.op("T", lambda e: e.matmul(PS[6][0:2, 128:256], ABP[:, :], IDENT[:, :],
                                         is_transpose=True, start=True, stop=True),
                 reads=[B("abp"), onceb], writes=[PSB[6]])
            yield
            s.op("S", lambda e: e.copy(ABROW[:, :], PS[6][0:2, 128:256]),
                 reads=[PSB[6]], writes=[B("abrow")])
            s.op("V", lambda e: e.tensor_tensor_scan(
                DROW[:, :], ABROW[0:1, :], ZEROS[0:1, :], 1.0, ALU.mult, ALU.add),
                reads=[B("abrow"), onceb], writes=[B("drow")])
            s.op("T", lambda e: e.matmul(PS[6][:, 256:384], ONES1[:, :], ABROW[0:1, :],
                                         start=True, stop=True),
                 reads=[B("abrow"), onceb], writes=[PSB[6]])
            yield
            s.op("V", lambda e: e.tensor_single_scalar(TMP2[:, :], PS[6][:, 256:384], 1.0, ALU.subtract),
                 reads=[PSB[6]], writes=[B("tmp2")])
            s.op("V", lambda e: e.tensor_tensor(TMP2[:, :], TMP2[:, :], MGT[:, :], ALU.mult),
                 reads=[B("tmp2"), onceb], writes=[B("tmp2")])
            s.op("V", lambda e: e.tensor_single_scalar(TMP2[:, :], TMP2[:, :], 1.0, ALU.add),
                 reads=[B("tmp2")], writes=[B("tmp2")])
            s.op("V", lambda e: e.tensor_tensor_scan(
                GAM[:, :], TMP2[:, :], ZEROS[:, :], 1.0, ALU.mult, ALU.add),
                reads=[B("tmp2"), onceb], writes=[B("gam")])
            s.op("G", lambda e: e.tensor_tensor(GAM[:, :], GAM[:, :], MGE[:, :], ALU.mult),
                 reads=[B("gam"), onceb], writes=[B("gam")])
            s.op("T", lambda e: e.matmul(PS[6][:, 384:385], ONES1[:, :], DROW[0:1, 127:128],
                                         start=True, stop=True),
                 reads=[B("drow"), onceb], writes=[PSB[6]])
            yield
            s.op("S", lambda e: e.copy(DLC[:, :], PS[6][:, 384:385]),
                 reads=[PSB[6]], writes=[B("dlc")])
            s.op("T", lambda e: e.matmul(PS[7][:, 0:128], GAM[:, :], IDENT[:, :],
                                         is_transpose=True, start=True, stop=True),
                 reads=[B("gam"), onceb], writes=[PSB[7]])
            yield
            s.op("V", lambda e: e.tensor_tensor(GAMA[:, :], PS[7][:, 0:128], IDENT[:, :], ALU.subtract),
                 reads=[PSB[7], onceb], writes=[B("gama")])
            s.op("T", lambda e: e.matmul(PS[7][:, 0:1], DROW[:, :], IDENT[0:1, 0:1],
                                         is_transpose=True, start=True, stop=True),
                 reads=[B("drow"), onceb], writes=[PSB[7]])
            yield
            s.op("S", lambda e: e.copy(DCOL[:, :], PS[7][:, 0:1]),
                 reads=[PSB[7]], writes=[B("dcol")])
            s.op("V", lambda e: e.tensor_scalar(NBD[:, :], DCOL[:, :], ABP[:, 1:2], -1.0, ALU.mult, ALU.mult),
                 reads=[B("dcol"), B("abp")], writes=[B("nbd")])
            s.op("V", lambda e: e.tensor_single_scalar(NBC[:, :], ABP[:, 1:2], -1.0, ALU.mult),
                 reads=[B("abp")], writes=[B("nbc")])
            yield

            s.op("T", lambda e: e.matmul(PS[7][:, 0:128], KTQT[:, 0:128], KTQT[:, 0:128],
                                         start=True, stop=True),
                 reads=[B("ktqt")], writes=[PSB[7]])
            yield
            s.op("V", lambda e: e.tensor_tensor(XW[:, :], PS[7][:, 0:128], GAMA[:, :], ALU.mult),
                 reads=[PSB[7], B("gama")], writes=[B("xw")])
            s.op("V", lambda e: e.tensor_single_scalar(XW[:, :], XW[:, :], NBC[:, :], ALU.mult),
                 reads=[B("xw"), B("nbc")], writes=[B("xw")])
            s.op("T", lambda e: e.matmul(PS[7][:, 0:128], XW[:, :], IDENT[:, :],
                                         is_transpose=True, start=True, stop=True),
                 reads=[B("xw"), onceb], writes=[PSB[7]])
            yield
            s.op("S", lambda e: e.copy(XWT[:, :], PS[7][:, 0:128]),
                 reads=[PSB[7]], writes=[B("xwt")])

            s.op("T", lambda e: e.matmul(PS[6][:, 0:128], KTQT[:, 0:128], STT[:, b * C:(b + 1) * C],
                                         start=True, stop=True),
                 reads=[B("ktqt"), stb[b]], writes=[PSB[6]])
            yield
            s.op("V", lambda e: e.tensor_single_scalar(TMP1[:, :], QKVG[:, 256:384], ABP[:, 1:2], ALU.mult),
                 reads=[B("qkvg"), B("abp")], writes=[B("tmp1")])
            s.op("V", lambda e: e.scalar_tensor_tensor(YV[:, :], PS[6][:, 0:128], NBD[:, :],
                                                       TMP1[:, :], ALU.mult, ALU.add),
                 reads=[PSB[6], B("nbd"), B("tmp1")], writes=[B("yv")])
            yield

            cx, cxt = B("xw"), B("xwt")
            cx_ap, cxt_ap = XW, XWT
            for lev in range(7):
                pb = 7 if lev % 2 == 0 else 6
                pb2 = 6 if lev % 2 == 0 else 7
                s.op("T", lambda e, pb=pb, cxt_ap=cxt_ap: e.matmul(
                    PS[pb][:, 0:128], cxt_ap[:, :], YV[:, :], start=True, stop=True),
                    reads=[cxt, B("yv")], writes=[PSB[pb]])
                yield
                s.op("V", lambda e, pb=pb: e.tensor_tensor(YV[:, :], YV[:, :], PS[pb][:, 0:128], ALU.add),
                     reads=[PSB[pb], B("yv")], writes=[B("yv")])
                if lev < 6:
                    par = lev % 2
                    nx, nxt = B(f"xp{par}"), B(f"xpt{par}")
                    nx_ap, nxt_ap = XP[par], XPT[par]
                    s.op("T", lambda e, pb2=pb2, cxt_ap=cxt_ap, cx_ap=cx_ap: e.matmul(
                        PS[pb2][:, 128:256], cxt_ap[:, :], cx_ap[:, :], start=True, stop=True),
                        reads=[cxt, cx], writes=[PSB[pb2]])
                    yield
                    s.op("S", lambda e, pb2=pb2, nx_ap=nx_ap: e.copy(nx_ap[:, :], PS[pb2][:, 128:256]),
                         reads=[PSB[pb2]], writes=[nx])
                    s.op("T", lambda e, pb2=pb2, cxt_ap=cxt_ap, cx_ap=cx_ap: e.matmul(
                        PS[pb2][:, 256:384], cx_ap[:, :], cxt_ap[:, :], start=True, stop=True),
                        reads=[cxt, cx], writes=[PSB[pb2]])
                    yield
                    s.op("S", lambda e, pb2=pb2, nxt_ap=nxt_ap: e.copy(nxt_ap[:, :], PS[pb2][:, 256:384]),
                         reads=[PSB[pb2]], writes=[nxt])
                    cx, cxt, cx_ap, cxt_ap = nx, nxt, nx_ap, nxt_ap
                if lev % 2 == 1:
                    yield

            s.op("T", lambda e: e.matmul(PS[7][:, 0:128], YV[:, :], IDENT[:, :],
                                         is_transpose=True, start=True, stop=True),
                 reads=[B("yv"), onceb], writes=[PSB[7]])
            yield
            s.op("S", lambda e: e.copy(VT[:, :], PS[7][:, 0:128]),
                 reads=[PSB[7]], writes=[B("vt")])
            yield

            s.op("T", lambda e: e.matmul(PS[7][:, 0:128], VT[:, :], KTQT[:, 128:256],
                                         start=True, stop=True),
                 reads=[B("vt"), B("ktqt")], writes=[PSB[7]])
            yield
            s.op("V", lambda e: e.tensor_tensor(WTT[:, :], PS[7][:, 0:128], GAM[:, :], ALU.mult),
                 reads=[PSB[7], B("gam")], writes=[B("wtt")])

            s.op("T", lambda e: e.matmul(PS[6][:, 0:128], WTT[:, :], KTL[:, :],
                                         start=True, stop=True),
                 reads=[B("wtt"), B("ktile")], writes=[PSB[6]])
            yield
            s.op("T", lambda e: e.matmul(PS[7][:, 128:256], KTQT[:, 128:256], SSS[:, b * C:(b + 1) * C],
                                         start=True, stop=True),
                 reads=[B("ktqt"), ssb[b]], writes=[PSB[7]])
            yield
            s.op("V", lambda e: e.tensor_single_scalar(OG[:, :], PS[7][:, 128:256], DCOL[:, :], ALU.mult),
                 reads=[PSB[7], B("dcol")], writes=[B("og")])
            s.op("V", lambda e: e.tensor_tensor(OG[:, :], OG[:, :], PS[6][:, 0:128], ALU.add),
                 reads=[PSB[6], B("og")], writes=[B("og")])
            s.op("V", lambda e: e.tensor_tensor(OG[:, :], OG[:, :], GT_[:, :], ALU.mult),
                 reads=[B("og"), B("gtile")], writes=[B("og")])
            yield

            s.op("V", lambda e: e.tensor_single_scalar(TMP1[:, :], YV[:, :], GAM[:, 127:128], ALU.mult),
                 reads=[B("yv"), B("gam")], writes=[B("tmp1")])
            s.op("T", lambda e: e.matmul(PS[6][:, 128:256], KTL[:, :], TMP1[:, :],
                                         start=True, stop=True),
                 reads=[B("ktile"), B("tmp1")], writes=[PSB[6]])
            yield
            s.op("V", lambda e: e.scalar_tensor_tensor(
                STT[:, b * C:(b + 1) * C], STT[:, b * C:(b + 1) * C], DLC[:, :],
                PS[6][:, 128:256], ALU.mult, ALU.add),
                reads=[PSB[6], B("dlc"), stb[b]], writes=[stb[b]])
            s.op("T", lambda e: e.matmul(PS[6][:, 256:384], TMP1[:, :], KTL[:, :],
                                         start=True, stop=True),
                 reads=[B("ktile"), B("tmp1")], writes=[PSB[6]])
            yield
            s.op("V", lambda e: e.scalar_tensor_tensor(
                SSS[:, b * C:(b + 1) * C], SSS[:, b * C:(b + 1) * C], DLC[:, :],
                PS[6][:, 256:384], ALU.mult, ALU.add),
                reads=[PSB[6], B("dlc"), ssb[b]], writes=[ssb[b]])
            yield

            if debug_bc == (b, c):
                for di, (ap_, bufs_) in enumerate([
                        (QKVG[:, 0:128], [B("qkvg")]), (KTL[:, :], [B("ktile")]),
                        (GAM[:, :], [B("gam")]), (YV[:, :], [B("yv")]),
                        (OG[:, :], [B("og")]), (GT_[:, :], [B("gtile")]),
                        (STT[:, b * C:(b + 1) * C], [stb[b]]),
                        (SSS[:, b * C:(b + 1) * C], [ssb[b]])]):
                    s.dma("SQ", "wps", lambda e, di=di, ap_=ap_: e.dma_start(
                        out=dbg[di], in_=ap_), reads=list(bufs_))
            s.op("T", lambda e: e.matmul(PS[7][:, 0:128], OG[:, :], IDENT[:, :],
                                         is_transpose=True, start=True, stop=True),
                 reads=[B("og"), onceb], writes=[PSB[7]])
            yield
            s.op("S", lambda e: e.copy(OT[:, :], PS[7][:, 0:128]),
                 reads=[PSB[7]], writes=[B("ot")])
            for h in range(2):
                pb = 6 + h
                s.op("T", lambda e, pb=pb, h=h: e.matmul(
                    PS[pb][:, 0:512], OT[:, :], WO[:, h * 512:(h + 1) * 512],
                    start=True, stop=True),
                    reads=[B("ot"), onceb], writes=[PSB[pb]])
                yield
                s.op("V", lambda e, pb=pb, h=h: e.tensor_tensor(
                    LNT[:, 0:512], PS[pb][:, 0:512], BO[:, h * 512:(h + 1) * 512], ALU.add),
                    reads=[PSB[pb], onceb], writes=[B("lnt")])
                s.op("V", lambda e, h=h, c=c: e.tensor_tensor(
                    Y1[:, c * D + h * 512: c * D + (h + 1) * 512],
                    LNT[:, 0:512],
                    Y1[:, c * D + h * 512: c * D + (h + 1) * 512], ALU.add),
                    reads=[B("lnt"), y1b[c]], writes=[y1b[c]])

        ic_no = [0]

        pasets = [
            dict(x=WPQ, xb="wpq", u=PAX, ub="pax", sg=PASGN, sgb="pasgn",
                 q=PASC, pool="papool", sm=4),
            dict(x=WPQ2, xb="wpq2", u=PAX2, ub="pax2", sg=PASGN2,
                 sgb="pasgn2", q=PASC2, pool="papool2", sm=8),
        ]

        def pa_chunk_gen(i, c):
            st = pasets[i % 2]
            xqt3 = XQT[:, :].rearrange("p (kc r) -> p kc r", r=FW)
            lnb = B(st["xb"])
            layernorm(Y1[:, c * D:(c + 1) * D], [y1b[c]], st["x"][:, :], [lnb],
                      sq_ap=st["sg"][:, :], sq_buf=B(st["sgb"]), small0=st["sm"])
            yield
            yield from qchain_gen(
                st["x"][:, :], [lnb], st["x"][:, :], [lnb], D,
                st["u"], B(st["ub"]), st["sg"], B(st["sgb"]), 0, st["pool"],
                q_aps=(st["q"][:, 0:64], st["q"][:, 64:128], st["q"][:, 128:192]))
            yield "TG"
            for g in range(2):
                tgroup([(st["x"][:, (g * 4 + j) * 128:(g * 4 + j + 1) * 128], [lnb])
                        for j in range(4)], 5,
                       [(xqt3[:, g * 4:(g + 1) * 4, i * 128:(i + 1) * 128],
                         [xqtb[i]], "S", 0, 512)])
                yield

        def phaseA_gen(b2, ph2):
            t0b = ph2 * FT
            gens = [pa_chunk_gen(i, t0b + i) for i in range(min(2, FT))]
            nxt_i = 2
            hold = []
            while gens:
                for g in list(gens):
                    r = next(g, "END")
                    if r == "TG":
                        gens.remove(g)
                        hold.append(g)
                        continue
                    if r == "END":
                        gens.remove(g)
                yield
            yield "TGREADY"
            for g in hold:
                for _ in g:
                    yield
            gens = [pa_chunk_gen(i, t0b + i) for i in range(2, FT)]
            while gens:
                for g in list(gens):
                    if next(g, "END") == "END":
                        gens.remove(g)
                yield

        def ffn_phase(b, ph, nxt_fn=None, inter=(), tick=None):
            if tick is None:
                tick = lambda n=1: 0
            t0 = ph * FT

            gph = ic_no[0]
            ic_no[0] += 1

            def par(ic):
                return (gph * NIC + ic) % 2

            def emit_load_gu(ic, pi=None):
                p_ = par(ic if pi is None else pi)
                gu = WSTR[:, p_ * 2048:(p_ + 1) * 2048]
                s.dma("LQ", wsem[p_], lambda e, ic=ic, gu=gu: e.dma_start(
                    out=gu.rearrange("p (kc n) -> p kc n", n=256),
                    in_=wgus[ic, :, :, :].rearrange("kc p n -> p kc n")),
                    reads=[wgusb], writes=[wslot[p_]])

            def emit_load_dn(ic):
                p_ = par(ic)
                dn = WSTR[:, 4096 + p_ * 1024: 4096 + (p_ + 1) * 1024]
                s.dma("LQ", f"dns{p_}", lambda e, ic=ic, dn=dn: e.dma_start(
                    out=dn, in_=wdqs[ic, :, :]),
                    reads=[wdqsb], writes=[dnslot[p_]])

            def emit_gateup(ic, zs=None, pi=None):
                p_ = par(ic if pi is None else pi)
                gu = WSTR[:, p_ * 2048:(p_ + 1) * 2048]
                hb = B(f"hbuf{p_}_lo")
                hsl = HBUF[:, p_ * FW:(p_ + 1) * FW]
                xr = [xqtb[i] for i in range(FT)]
                for half, bank in ((0, 0), (1, 1)):
                    for kc in range(NKC):
                        s.op("T", lambda e, kc=kc, gu=gu, bank=bank, half=half: e.matmul(
                            PS[bank][:, 0:FW],
                            gu[:, kc * 256 + half * 128: kc * 256 + half * 128 + 128],
                            XQT[:, kc * FW:(kc + 1) * FW],
                            start=(kc == 0), stop=(kc == NKC - 1)),
                            reads=xr + [wslot[p_]], writes=[PSB[bank]], accum=(kc > 0))
                        if zs is not None:
                            zs()
                        if kc % 2 == 1:
                            tick(1)
                s.op("S", lambda e, hsl=hsl: e.activation(hsl, PS[0][:, 0:FW], ACT.Silu),
                     reads=[PSB[0]], writes=[hb])
                s.op("V", lambda e, hsl=hsl: e.tensor_tensor(hsl, hsl, PS[1][:, 0:FW], ALU.mult),
                     reads=[PSB[1], hb], writes=[hb])
                if zs is not None:
                    for _ in range(4):
                        zs()
                tick(1)

            def hquant_gen(ic):
                p_ = par(ic)
                p3 = ic % 3
                hb = B(f"hbuf{p_}_lo")
                hsl = HBUF[:, p_ * FW:(p_ + 1) * FW]
                hq = HQT[:, p3 * FW:(p3 + 1) * FW]
                qb = B("quantpool")
                axb, sgb = B("axt"), B("sgn")
                hqb = B(f"hqt{p3}")
                for i in range(FT):
                    s.op("T", lambda e, i=i: e.matmul(
                        PS[2][:, i * 128:(i + 1) * 128], hsl[:, i * 128:(i + 1) * 128],
                        IDENT[:, :], is_transpose=True, start=(i == 0), stop=(i == FT - 1)),
                        reads=[hb, onceb], writes=[PSB[2]], accum=(i > 0))
                    yield
                s.op("S", lambda e: e.copy(hq, PS[2][:, 0:FW]),
                     reads=[PSB[2]], writes=[hqb])
                yield
                s.op("S", lambda e: e.sign(SGN[:, 512:512 + FW], hsl), reads=[hb], writes=[sgb])
                yield "hfree"
                s.op("S", lambda e: e.activation(AXT[:, 0:FW], hq, ACT.Abs),
                     reads=[hqb], writes=[axb])
                yield
                s.op("V", lambda e: e.tensor_reduce(
                    QSC[:, 0:NGR], AXT[:, 0:FW].rearrange("p (g e) -> p g e", e=16),
                    AX.X, ALU.max), reads=[axb], writes=[qb])
                yield
                c6 = 0.16666667163372039794921875
                s.op("V", lambda e: e.tensor_single_scalar(QS2[:, 0:NGR], QSC[:, 0:NGR], c6, ALU.mult),
                     reads=[qb], writes=[qb])
                s.op("V", lambda e: e.tensor_single_scalar(QS2[:, 32:32 + NGR], QS2[:, 0:NGR], 6.0, ALU.mult),
                     reads=[qb], writes=[qb])
                yield
                s.op("V", lambda e: e.tensor_tensor(QS2[:, 32:32 + NGR], QSC[:, 0:NGR], QS2[:, 32:32 + NGR], ALU.subtract),
                     reads=[qb], writes=[qb])
                s.op("V", lambda e: e.tensor_single_scalar(QS2[:, 32:32 + NGR], QS2[:, 32:32 + NGR], c6, ALU.mult),
                     reads=[qb], writes=[qb])
                yield
                s.op("V", lambda e: e.tensor_tensor(QSC[:, 0:NGR], QS2[:, 0:NGR], QS2[:, 32:32 + NGR], ALU.add),
                     reads=[qb], writes=[qb])
                s.op("V", lambda e: e.tensor_single_scalar(QSC[:, 0:NGR], QSC[:, 0:NGR], 1e-12, ALU.max),
                     reads=[qb], writes=[qb])
                yield
                s.op("V", lambda e: e.reciprocal(QS2[:, 0:NGR], QSC[:, 0:NGR]),
                     reads=[qb], writes=[qb])
                s.op("V", lambda e: e.tensor_single_scalar(QSC[:, 0:NGR], QSC[:, 0:NGR], 0.5, ALU.mult),
                     reads=[qb], writes=[qb])
                yield
                rb = QS2[:, 0:NGR].unsqueeze(2).broadcast_to([128, NGR, 16])
                s.op("G", lambda e: e.tensor_tensor(
                    AXT[:, 0:FW].rearrange("p (g e) -> p g e", e=16),
                    AXT[:, 0:FW].rearrange("p (g e) -> p g e", e=16), rb, ALU.mult),
                    reads=[axb, qb], writes=[axb])
                yield
                u = AXT[:, 0:FW]
                v = SGN[:, 0:FW]
                s.op("V", lambda e: e.tensor_scalar(v, u, QTH[6], 2.0, ALU.is_gt, ALU.mult),
                     reads=[axb], writes=[sgb])
                yield
                for th in (QTH[5], QTH[4]):
                    s.op("V", lambda e, th=th: e.scalar_tensor_tensor(
                        v, u, th, v, ALU.is_gt, ALU.add), reads=[axb, sgb], writes=[sgb])
                    yield
                s.op("V", lambda e: e.tensor_single_scalar(v, v, 2.0, ALU.mult),
                     reads=[sgb], writes=[sgb])
                yield
                for th in (QTH[3], QTH[2], QTH[1], QTH[0]):
                    s.op("V", lambda e, th=th: e.scalar_tensor_tensor(
                        v, u, th, v, ALU.is_gt, ALU.add), reads=[axb, sgb], writes=[sgb])
                    yield
                sb_ = QSC[:, 0:NGR].unsqueeze(2).broadcast_to([128, NGR, 16])
                s.op("G", lambda e: e.tensor_tensor(
                    v.rearrange("p (g e) -> p g e", e=16),
                    v.rearrange("p (g e) -> p g e", e=16), sb_, ALU.mult),
                    reads=[qb, sgb], writes=[sgb])
                yield
                for i in range(FT):
                    s.op("T", lambda e, i=i: e.matmul(
                        PS[2][:, i * 128:(i + 1) * 128], v[:, i * 128:(i + 1) * 128],
                        IDENT[:, :], is_transpose=True, start=(i == 0), stop=(i == FT - 1)),
                        reads=[sgb, onceb], writes=[PSB[2]], accum=(i > 0))
                    yield
                s.op("S", lambda e, hq=hq: e.copy(hq, PS[2][:, 0:FW]),
                     reads=[PSB[2]], writes=[hqb])
                yield
                s.op("G", lambda e, hq=hq: e.tensor_tensor(hq, hq, SGN[:, 512:512 + FW], ALU.mult),
                     reads=[sgb, hqb], writes=[hqb])

            def emit_tail_pair(j0, j1):
                js = [j0] + ([j1] if j1 is not None else [])
                for i in range(FT):
                    c = t0 + i
                    for k, j_ in enumerate(js):
                        p_ = par(j_)
                        p3 = j_ % 3
                        dn = WSTR[:, 4096 + p_ * 1024: 4096 + (p_ + 1) * 1024]
                        for h in range(2):
                            pb = 3 + h
                            s.op("T", lambda e, pb=pb, i=i, h=h, dn=dn, p3=p3,
                                 st=(k == 0), sp=(k == len(js) - 1): e.matmul(
                                PS[pb][:, 0:512], HQT[:, p3 * FW + i * 128: p3 * FW + (i + 1) * 128],
                                dn[:, h * 512:(h + 1) * 512], start=st, stop=sp),
                                reads=[B(f"hqt{p3}"), dnslot[p_]], writes=[PSB[pb]],
                                accum=(k > 0))
                    for h in range(2):
                        pb = 3 + h
                        s.op("V", lambda e, pb=pb, h=h, c=c: e.tensor_tensor(
                            Y1[:, c * D + h * 512: c * D + (h + 1) * 512],
                            Y1[:, c * D + h * 512: c * D + (h + 1) * 512],
                            PS[pb][:, 0:512], ALU.add),
                            reads=[PSB[pb], y1b[c]], writes=[y1b[c]])
                    tick(1)

            def drain(gen):
                if gen is not None:
                    for _ in gen:
                        pass



            pre0 = gph > 0
            if not pre0:
                emit_load_gu(0)
            emit_load_dn(0)
            emit_load_dn(1)
            qgs = []
            pac = [None, False]

            def zipstep():
                while qgs:
                    r = next(qgs[0], "END")
                    if r == "END":
                        qgs.pop(0)
                        continue
                    return

            def pa_step():
                if pac[0] is None or pac[1]:
                    return
                r = next(pac[0], "END")
                if r == "END":
                    pac[0] = None
                elif r == "TGREADY":
                    pac[1] = True

            def zs2():
                zipstep()
                pa_step()

            for ic in range(NIC):
                if not (ic == 0 and pre0):
                    emit_gateup(ic, zs2)
                if ic + 1 < NIC:
                    emit_load_gu(ic + 1)
                if ic == NIC - 1 and nxt_fn is not None:
                    emit_load_gu(0, pi=NIC)
                if ic >= 1:
                    while qgs:
                        drain(qgs.pop(0))
                    if ic % 2 == 0:
                        emit_tail_pair(ic - 2, ic - 1)
                        emit_load_dn(ic)
                        emit_load_dn(ic + 1)
                if ic == NIC - 3 and nxt_fn is not None:
                    pac[0] = nxt_fn()
                qgs.append(hquant_gen(ic))
                for _ in range(6):
                    zs2()
            pac[1] = False
            while qgs or pac[0] is not None:
                if qgs:
                    zipstep()
                pa_step()
            if nxt_fn is not None:
                emit_gateup(0, pi=NIC)
            emit_tail_pair(NIC - 2, NIC - 1)

            for i in range(FT):
                c = t0 + i
                s.dma("SQ", f"st{c}", lambda e, b=b, c=c: e.dma_start(
                    out=y_out[b, c * 128:(c + 1) * 128, :],
                    in_=Y1[:, c * D:(c + 1) * D]), reads=[y1b[c]])

        AQ = [[0, c, attn_chunk_gen(0, c)] for c in range(NCH)]

        def advance(n):
            done = 0
            while AQ and done < n:
                try:
                    next(AQ[0][2])
                    done += 1
                except StopIteration:
                    AQ.pop(0)
            return done

        def drain_until(b, cmax):
            while AQ and (AQ[0][0] < b or (AQ[0][0] == b and AQ[0][1] <= cmax)):
                advance(10 ** 6)

        wq = []
        for k, u in enumerate(wprep_units):
            if u[0] == "gu":
                wq.append(wprep_gateup_gen(k, u[1], u[2], u[3]))
            else:
                wq.append(wprep_down_gen(k, u[1], u[2], u[3]))
        act = []
        while wq or act:
            while len(act) < 2 and wq:
                act.append(wq.pop(0))
            for g in list(act):
                if next(g, "END") == "END":
                    act.remove(g)
            advance(1)

        def mk_nxt(nb, nph):
            def f():
                drain_until(nb, (nph + 1) * FT - 1)
                return phaseA_gen(nb, nph)
            return f

        drain_until(0, FT - 1)
        for _ in phaseA_gen(0, 0):
            advance(1)

        for b in range(NB):
            for ph in range(NPH):
                drain_until(b, (ph + 1) * FT - 1)
                if ph + 1 < NPH:
                    nxt = mk_nxt(b, ph + 1)
                elif b + 1 < NB:
                    nxt = mk_nxt(b + 1, 0)
                else:
                    nxt = None
                tkc = [0]

                def _tick(n=1, tkc=tkc):
                    tkc[0] += 1
                    advance(n + (1 if tkc[0] <= 14 else 0))

                ffn_phase(b, ph, nxt, AQ, _tick)
                if b + 1 < NB:
                    AQ.extend([[b + 1, c, attn_chunk_gen(b + 1, c)]
                               for c in range(ph * FT, (ph + 1) * FT)])
        advance(10 ** 9)

        s.raw_wait_all("S", "wps")
        for c in range(NCH):
            s.raw_wait_all("S", f"st{c}")

    with nc.Block() as block:
        s.emit(block)
    return nc


def host_inputs(xs, w, ft=4):
    return {
        "x": np.ascontiguousarray(xs),
        "wcat": np.ascontiguousarray(
            np.concatenate([w["Wq"], w["Wk"], w["Wv"], w["Wg"]], axis=1)),
        "wab": np.ascontiguousarray(np.concatenate([w["Wa"], w["Wb"]], axis=1)),
        "wo": np.ascontiguousarray(w["Wo"]),
        "bqkvg": np.ascontiguousarray(np.tile(
            np.concatenate([w["bq"], w["bk"], w["bv"], w["bg"]])[None, :], (128, 1))),
        "bab": np.ascontiguousarray(np.tile(
            np.concatenate([w["ba"], w["bb"]])[None, :], (128, 1))),
        "bo": np.ascontiguousarray(np.tile(w["bo"][None, :], (128, 1))),
        "wgate": np.ascontiguousarray(w["w_gate"]),
        "wup": np.ascontiguousarray(w["w_up"]),
        "wdown": np.ascontiguousarray(w["w_down"]),
        "ident": np.eye(128, dtype=np.float32),
        "mask_gt": (np.arange(128)[None, :] > np.arange(128)[:, None]).astype(np.float32),
        "mask_ge": (np.arange(128)[None, :] >= np.arange(128)[:, None]).astype(np.float32),
        "ones1": np.ones((1, 128), dtype=np.float32),
    }


def pack_weights(L0_attn_Wq, L0_attn_bq, L0_attn_Wk, L0_attn_bk, L0_attn_Wv,
                 L0_attn_bv, L0_attn_Wa, L0_attn_ba, L0_attn_Wb, L0_attn_bb,
                 L0_attn_Wg, L0_attn_bg, L0_attn_Wo, L0_attn_bo,
                 L0_ffn_w_gate, L0_ffn_w_up, L0_ffn_w_down):
    f = lambda a: np.asarray(a, np.float32)
    return {
        "Wq": f(L0_attn_Wq), "bq": f(L0_attn_bq), "Wk": f(L0_attn_Wk), "bk": f(L0_attn_bk),
        "Wv": f(L0_attn_Wv), "bv": f(L0_attn_bv), "Wa": f(L0_attn_Wa), "ba": f(L0_attn_ba),
        "Wb": f(L0_attn_Wb), "bb": f(L0_attn_bb), "Wg": f(L0_attn_Wg), "bg": f(L0_attn_bg),
        "Wo": f(L0_attn_Wo), "bo": f(L0_attn_bo),
        "w_gate": f(L0_ffn_w_gate), "w_up": f(L0_ffn_w_up), "w_down": f(L0_ffn_w_down),
    }


def kernel(L0_attn_x, **kw):
    x = np.asarray(L0_attn_x, dtype=np.float32)
    Bfull, T, _ = x.shape
    n_cores = 8
    nb = Bfull // n_cores
    w = pack_weights(**kw)
    nc = build_program(NB=nb, T=T, FT=4)
    in_maps = [host_inputs(x[c * nb:(c + 1) * nb], w, 4) for c in range(n_cores)]
    res = run_bass_kernel_spmd(nc, in_maps, core_ids=list(range(n_cores)))
    return np.concatenate([res.results[c]["y"] for c in range(n_cores)], axis=0)
